# version 16
# baseline (speedup 1.0000x reference)
import math
import numpy as np
from contextlib import ExitStack

import concourse.bass as bass
import concourse.mybir as mybir
from concourse.bass_utils import run_bass_kernel_spmd

F32 = mybir.dt.float32
BF16 = mybir.dt.bfloat16
I32 = mybir.dt.int32
AF = mybir.ActivationFunctionType
ALU = mybir.AluOpType
AX = mybir.AxisListType

D = 4096
S = 8192
NC = 8
TPC = S // NC
TT = 512
HPC = 4
HD = 128
NCH = D // 128
ISQ = 1.0 / math.sqrt(HD)
EPS = 1e-5

_DEV_STAGE = None

ENGS = ("sync", "tensor", "vector", "scalar", "gpsimd")


class P:
    def __init__(self, nc, es):
        self.nc = nc
        self.es = es
        self.streams = {n: [] for n in ENGS}
        self._sems = {}
        self.cnt = {}
        self.state = {}

    def sem(self, name):
        if name not in self._sems:
            self._sems[name] = self.es.enter_context(self.nc.semaphore(name))
            self.cnt[name] = 0
        return self._sems[name]

    def inc(self, name, k=1):
        self.sem(name)
        self.cnt[name] += k
        return self.cnt[name]

    def val(self, name):
        self.sem(name)
        return self.cnt[name]

    def op(self, eng, meth, *args, waits=(), inc=None, **kw):
        wl = [(self.sem(n), v) for (n, v) in waits if v > 0]
        after = None
        sem_i = None
        k_i = 0
        if inc is not None:
            after = self.inc(inc[0], inc[1])
            sem_i = self._sems[inc[0]]
            k_i = inc[1]

        def f(e, meth=meth, args=args, kw=kw, wl=wl, sem_i=sem_i, k_i=k_i):
            for s_, v_ in wl:
                e.wait_ge(s_, v_)
            ins = getattr(e, meth)(*args, **kw)
            if sem_i is not None:
                ins.then_inc(sem_i, k_i)
        self.streams[eng].append(f)
        return after

    def raw(self, eng, fn):
        self.streams[eng].append(fn)

    def wait(self, eng, *waits):
        wl = [(self.sem(n), v) for (n, v) in waits if v > 0]

        def f(e, wl=wl):
            for s_, v_ in wl:
                e.wait_ge(s_, v_)
        self.streams[eng].append(f)

    def dma(self, eng, out, in_, sem, waits=(), **kw):
        return self.op(eng, "dma_start", waits=waits, inc=(sem, 16), out=out, in_=in_, **kw)

    def fence(self):
        name = "fence"
        self.sem(name)
        base = self.cnt[name]
        for eng in ENGS:
            meth = "engine_nop" if eng in ("vector", "gpsimd") else "nop"
            self.op(eng, meth, inc=(name, 1))
        target = base + len(ENGS)
        for eng in ENGS:
            self.wait(eng, (name, target))
        return target


def build_program():
    nc = bass.Bass("TRN2", target_bir_lowering=False)

    xT = nc.dram_tensor("xT", [D, TPC], F32, kind="ExternalInput").ap()
    wq_t = nc.dram_tensor("wq_t", [D, HPC * HD], F32, kind="ExternalInput").ap()
    wk_t = nc.dram_tensor("wk_t", [D, HPC * HD], F32, kind="ExternalInput").ap()
    wv_t = nc.dram_tensor("wv_t", [D, HPC * HD], F32, kind="ExternalInput").ap()
    wo_tc = nc.dram_tensor("wo_tc", [NCH, D, 128], F32, kind="ExternalInput").ap()
    wf_tc = nc.dram_tensor("wf_tc", [NCH, D, 128], F32, kind="ExternalInput").ap()
    gate_t = nc.dram_tensor("gate_t", [D, 8], F32, kind="ExternalInput").ap()
    icsM = nc.dram_tensor("icsM", [8, D], F32, kind="ExternalInput").ap()
    ocsM = nc.dram_tensor("ocsM", [8, D], F32, kind="ExternalInput").ap()
    bq_f = nc.dram_tensor("bq_f", [128, HPC], F32, kind="ExternalInput").ap()
    bk_f = nc.dram_tensor("bk_f", [128, HPC], F32, kind="ExternalInput").ap()
    bv_r = nc.dram_tensor("bv_r", [1, HPC * HD], F32, kind="ExternalInput").ap()
    bo_f = nc.dram_tensor("bo_f", [128, NCH], F32, kind="ExternalInput").ap()
    bias_f = nc.dram_tensor("bias_f", [128, NCH], F32, kind="ExternalInput").ap()
    cid = nc.dram_tensor("cid", [1, 1], I32, kind="ExternalInput").ap()
    yT = nc.dram_tensor("yT", [D, TPC], F32, kind="ExternalOutput").ap()
    if _DEV_STAGE == "full_dbg":
        dbg_xn3 = nc.dram_tensor("dbg_xn3", [2, 128, NCH * TT], BF16, kind="ExternalOutput").ap()
        dbg_rt = nc.dram_tensor("dbg_rt", [2, 8, TT], BF16, kind="ExternalOutput").ap()
        dbg_x3 = nc.dram_tensor("dbg_x3", [2, 128, NCH * TT], F32, kind="ExternalOutput").ap()
    if _DEV_STAGE == "attn_dbg":
        dbg_nc = nc.dram_tensor("dbg_nc", [HPC, 1], F32, kind="ExternalOutput").ap()
        dbg_nrm = nc.dram_tensor("dbg_nrm", [HPC, 2, 20], F32, kind="ExternalOutput").ap()
        dbg_qsq = nc.dram_tensor("dbg_qsq", [HPC, HD, 512], BF16, kind="ExternalOutput").ap()
        dbg_qslot = nc.dram_tensor("dbg_qslot", [HPC, HD, 512], BF16, kind="ExternalOutput").ap()
        dbg_den = nc.dram_tensor("dbg_den", [HPC, 16, 512], F32, kind="ExternalOutput").ap()
        dbg_q = nc.dram_tensor("dbg_q", [HPC, HD, S], BF16, kind="ExternalOutput").ap()
        dbg_k = nc.dram_tensor("dbg_k", [HPC, HD, S], BF16, kind="ExternalOutput").ap()
        dbg_v = nc.dram_tensor("dbg_v", [HPC, S, HD], BF16, kind="ExternalOutput").ap()
        dbg_ain = nc.dram_tensor("dbg_ain", [HPC * HD, S], BF16, kind="ExternalOutput").ap()
        dbg_aout = nc.dram_tensor("dbg_aout", [HPC, NC * HD, S], BF16, kind="ExternalOutput").ap()
        dbg_xn2 = nc.dram_tensor("dbg_xn2", [NC * D, TPC], BF16, kind="ExternalOutput").ap()

    sgn_c = nc.dram_tensor("sgn_c", [NCH, D, 128], BF16).ap()
    xsp = nc.dram_tensor("xsp", [D, TPC], F32).ap()
    agx_in = [nc.dram_tensor("agx_in%d" % b, [D, TT], BF16).ap() for b in range(2)]
    agx_out = [nc.dram_tensor("agx_out%d" % b, [NC * D, TT], BF16).ap() for b in range(2)]
    qsp = nc.dram_tensor("qsp", [HPC, HD, S], BF16).ap()
    ksp = nc.dram_tensor("ksp", [HPC, HD, S], BF16).ap()
    vsp = nc.dram_tensor("vsp", [HPC, S, HD], BF16).ap()
    aga_in = nc.dram_tensor("aga_in", [HPC * HD, S], BF16).ap()
    aga_out = nc.dram_tensor("aga_out", [HPC, NC * HD, S], BF16).ap()

    es = ExitStack()
    p = P(nc, es)

    sb = nc.alloc_sbuf_tensor
    ones_col = sb("ones_col", [128, 1], BF16)
    ones_row_f = sb("ones_row_f", [1, 128], F32)
    ident = sb("ident", [128, 128], BF16)
    onesM = sb("onesM", [128, 128], BF16)
    gate_sb = sb("gate_sb", [128, NCH * 8], BF16)
    ics_sb = sb("ics_sb", [8, D], BF16)
    ocs_sb = sb("ocs_sb", [8, D], BF16)
    bq_sb = sb("bq_sb", [128, HPC], F32)
    bk_sb = sb("bk_sb", [128, HPC], F32)
    bo_sb = sb("bo_sb", [128, NCH], F32)
    hbias_sb = sb("hbias_sb", [128, NCH], F32)
    bv_row = sb("bv_row", [1, HPC * HD], F32)
    bvb_sb = sb("bvb_sb", [128, HPC * HD], BF16)
    cid_sb = sb("cid_sb", [1, 1], I32)
    rowsA = sb("rowsA", [1, 1536], F32)
    rowsB = sb("rowsB", [1, 1536], F32)
    routing = sb("routing", [128, 32], BF16)
    routingT = sb("routingT", [8, TT], BF16)
    rsc1 = sb("rsc1", [128, 32], F32)
    rsc2 = sb("rsc2", [128, 32], F32)
    rsc3 = sb("rsc3", [128, 8], F32)
    tmpf1 = sb("tmpf1", [128, 512], F32)
    tmpf2 = sb("tmpf2", [128, 512], F32)
    osc_sb = sb("osc_sb", [128, 3 * 512], BF16)
    negC = sb("negC", [128, 1], F32)
    stq = sb("stq", [128, 2 * 512], BF16)
    stk = sb("stk", [128, 2 * 512], BF16)
    stv = sb("stv", [128, 2 * 512], BF16)

    XF = sb("XF", [128, NCH * TT], F32)
    XN = sb("XN", [128, NCH * TT], BF16)
    XSQ = sb("XSQ", [128, NCH * TT], BF16)
    WST = sb("WST", [128, 2048], F32)
    SGN = sb("SGN", [128, 3 * D], BF16)

    XFb = XF[:].bitcast(BF16)
    WSTb = WST[:].bitcast(BF16)

    PSQ = [nc.alloc_psum_tensor("psq%d" % i, [128, 1024], F32) for i in range(4)]
    PS = [PSQ[i // 2][:, (i % 2) * 512:(i % 2 + 1) * 512] for i in range(8)]

    def phase0():
        p.dma("sync", bq_sb[:], bq_f, "dC")
        p.dma("sync", bk_sb[:], bk_f, "dC")
        p.dma("sync", bo_sb[:], bo_f, "dC")
        p.dma("sync", hbias_sb[:], bias_f, "dC")
        p.dma("sync", cid_sb[:], cid, "dC")
        c_dc = p.dma("sync", bv_row[:], bv_r, "dC")
        p.dma("gpsimd", gate_sb[:].rearrange("p (c e) -> p c e", e=8),
              gate_t.rearrange("(c p) e -> p c e", p=128), "dG")
        p.dma("gpsimd", ics_sb[:], icsM, "dG")
        c_dg = p.dma("gpsimd", ocs_sb[:], ocsM, "dG")
        p.op("vector", "memset", ones_col[:], 1.0)
        c_ones = p.op("vector", "memset", ones_row_f[:], 1.0, inc=("sv", 1))
        p.op("gpsimd", "memset", ident[:], 0.0)
        p.op("gpsimd", "affine_select", ident[:], ident[:], [[-1, 128]],
             ALU.not_equal, 1.0, base=0, channel_multiplier=1)
        p.op("gpsimd", "memset", onesM[:], 1.0)
        c_id = p.op("gpsimd", "affine_select", onesM[:], onesM[:], [[-1, 128]],
                    ALU.is_ge, 0.0, base=0, channel_multiplier=0, inc=("sg", 1))
        p.op("scalar", "nop", waits=[("dC", c_dc)], inc=("ss", 1))
        bc = PS[0][:, 0:HPC * HD]
        c_mm = p.op("tensor", "matmul", bc, ones_row_f[:], bv_row[:],
                    start=True, stop=True,
                    waits=[("dC", c_dc), ("sv", c_ones)], inc=("st", 1))
        p.op("scalar", "activation", bvb_sb[:], bc, AF.Copy,
             waits=[("st", c_mm)], inc=("ss", 1))
        p.wait("sync", ("dC", c_dc))

        def f(e):
            p.state["cid"] = e.value_load(cid_sb[0:1, 0:1])
        p.raw("sync", f)
        p.wait("gpsimd", ("dG", c_dg))
        return {"c_id": c_id}

    def emit_ln(first_waits_v=(), group_waits=None, sq_waits=()):
        r1 = PS[2][0:1, :]
        r2 = PS[3][0:1, :]
        bca = PS[4]
        bcb = PS[5]
        gw = group_waits or [list(first_waits_v)] + [[]] * 3
        c_cast = {}
        c_sq = {}
        for g in range(4):
            cs = 8 * g * TT
            ce = 8 * (g + 1) * TT
            c_cast[g] = p.op("vector", "tensor_copy", XN[:, cs:ce], XF[:, cs:ce],
                             waits=list(gw[g]), inc=("sv", 1))
            wsq = [("sv", c_cast[g])] + (list(sq_waits) if g == 0 else [])
            c_sq[g] = p.op("scalar", "activation", XSQ[:, cs:ce], XN[:, cs:ce],
                           AF.Square, waits=wsq, inc=("ss", 1))
        for c in range(NCH):
            w = [("sv", c_cast[c // 8])] if c % 8 == 0 else []
            p.op("tensor", "matmul", r1, ones_col[:], XN[:, c * TT:(c + 1) * TT],
                 start=(c == 0), stop=(c == NCH - 1), waits=w)
        for c in range(NCH):
            w = [("ss", c_sq[c // 8])] if c % 8 == 0 else []
            inc = ("st", 1) if c == NCH - 1 else None
            c_stats = p.op("tensor", "matmul", r2, ones_col[:], XSQ[:, c * TT:(c + 1) * TT],
                           start=(c == 0), stop=(c == NCH - 1), waits=w, inc=inc)
        mu = rowsA[:, 0:TT]
        mu = rowsA[:, 0:TT]
        m2 = rowsA[:, TT:2 * TT]
        var = rowsB[:, 0:TT]
        a_row = rowsB[:, TT:2 * TT]
        p.op("vector", "tensor_scalar_mul", mu, r1, 1.0 / D, waits=[("st", c_stats)])
        c0 = p.op("vector", "tensor_scalar_mul", m2, r2, 1.0 / D, inc=("sv", 1))
        p.wait("vector", ("sv", c0))
        c0 = p.op("vector", "tensor_mul", var, mu, mu, inc=("sv", 1))
        p.wait("vector", ("sv", c0))
        c0 = p.op("vector", "tensor_sub", var, m2, var, inc=("sv", 1))
        p.wait("vector", ("sv", c0))
        c_var = p.op("vector", "tensor_scalar_add", var, var, EPS, inc=("sv", 1))
        c_ln1 = p.op("scalar", "activation", var, var, AF.Ln,
                     waits=[("sv", c_var)], inc=("ss", 1))
        p.wait("scalar", ("ss", c_ln1))
        c_a = p.op("scalar", "activation", a_row, var, AF.Exp, scale=-0.5,
                   inc=("ss", 1))
        c0 = p.op("vector", "tensor_mul", mu, mu, a_row, waits=[("ss", c_a)],
                  inc=("sv", 1))
        p.wait("vector", ("sv", c0))
        c_b = p.op("vector", "tensor_scalar_mul", mu, mu, -1.0, inc=("sv", 1))
        p.op("tensor", "matmul", bca[:], ones_row_f[:], a_row,
             start=True, stop=True, waits=[("sv", c_b)])
        c_bc = p.op("tensor", "matmul", bcb[:], ones_row_f[:], mu,
                    start=True, stop=True, inc=("st", 1))
        marks = {}
        for c in range(NCH):
            w = [("st", c_bc)] if c == 0 else []
            p.op("vector", "tensor_tensor", tmpf1[:], XF[:, c * TT:(c + 1) * TT],
                 bca[:], ALU.mult, waits=w)
            inc = ("sv", 1) if c % 8 == 7 else None
            mk = p.op("vector", "tensor_tensor", XN[:, c * TT:(c + 1) * TT],
                      tmpf1[:], bcb[:], ALU.add, inc=inc)
            if c % 8 == 7:
                marks[c] = mk
        return marks

    def emit_router(ln_marks, c_id):
        lg_ps = PS[6][:, 0:32]
        rt_ps = PS[7][:].bitcast(BF16)[0:8, 0:TT]
        last_ln = ln_marks[NCH - 1]
        for ts in range(4):
            for i in range(NCH):
                w = [("sv", last_ln)] if (ts == 0 and i == 0) else []
                inc = ("st", 1) if (ts == 3 and i == NCH - 1) else None
                c_lg = p.op("tensor", "matmul", lg_ps[:, ts * 8:(ts + 1) * 8],
                            XN[:, i * TT + ts * 128: i * TT + (ts + 1) * 128],
                            gate_sb[:, i * 8:(i + 1) * 8],
                            start=(i == 0), stop=(i == NCH - 1), waits=w, inc=inc)
        c_mx = p.op("vector", "reduce_max", rsc3[:, 0:4],
                    lg_ps.rearrange("p (g e) -> p g e", e=8), AX.X,
                    waits=[("st", c_lg)], inc=("sv", 1))
        p.wait("vector", ("sv", c_mx))
        for g in range(4):
            inc = ("sv", 1) if g == 3 else None
            c_sh = p.op("vector", "tensor_scalar", rsc1[:, g * 8:(g + 1) * 8],
                        lg_ps[:, g * 8:(g + 1) * 8], rsc3[:, g:g + 1], None,
                        ALU.subtract, inc=inc)
        c_ex = p.op("scalar", "activation", rsc2[:], rsc1[:], AF.Exp,
                    waits=[("sv", c_sh)], inc=("ss", 1))
        c_sm = p.op("vector", "reduce_sum", rsc3[:, 0:4],
                    rsc2[:].rearrange("p (g e) -> p g e", e=8), AX.X,
                    waits=[("ss", c_ex)], inc=("sv", 1))
        p.wait("vector", ("sv", c_sm))
        c_rc = p.op("vector", "reciprocal", rsc3[:, 4:8], rsc3[:, 0:4], inc=("sv", 1))
        p.wait("vector", ("sv", c_rc))
        for g in range(4):
            inc = ("sv", 1) if g == 3 else None
            c_rt = p.op("vector", "tensor_scalar", routing[:, g * 8:(g + 1) * 8],
                        rsc2[:, g * 8:(g + 1) * 8], rsc3[:, 4 + g:5 + g], None,
                        ALU.mult, inc=inc)
        for g in range(4):
            w = [("sv", c_rt), ("sg", c_id)] if g == 0 else []
            inc = ("st", 1) if g == 3 else None
            c_tr = p.op("tensor", "transpose", rt_ps[:, g * 128:(g + 1) * 128],
                        routing[:, g * 8:(g + 1) * 8], ident[:], waits=w, inc=inc)
        c_rT = p.op("scalar", "activation", routingT[:], rt_ps, AF.Copy,
                    waits=[("st", c_tr)], inc=("ss", 1))
        return c_rT

    def emit_sign_chunk(o, w_sgn_free):
        spar = o % 3
        w_ld = list(w_sgn_free) + [("dSC%d" % spar, p.val("dSC%d" % spar))]
        slot = SGN[:, spar * D:(spar + 1) * D]
        c_ld = p.dma("gpsimd", slot.rearrange("p (c f) -> p c f", f=128),
                     wf_tc[o].rearrange("(c p) f -> p c f", p=128),
                     "dWf%d" % spar, waits=w_ld)
        p.op("vector", "tensor_scalar", WSTb[:, 0:D], slot, 0.0, None, ALU.is_lt,
             waits=[("dWf%d" % spar, c_ld)])
        c_sg = p.op("vector", "scalar_tensor_tensor", slot, slot, 0.0,
                    WSTb[:, 0:D], ALU.is_gt, ALU.subtract, inc=("svg", 1))
        p.dma("sync", sgn_c[o].rearrange("(c p) f -> p c f", p=128),
              slot.rearrange("p (c f) -> p c f", f=128),
              "dSC%d" % spar, waits=[("svg", c_sg)])
        return c_sg

    def emit_ffn(c_routingT, ln_marks, first_ffn, sgn_free0, sgn_free1=None, h_dst_free=()):
        sc_ps = [PS[0], PS[1]]
        y_ps = [PS[2], PS[3], PS[6]]
        os_ps = [PS[4], PS[5], PS[7]]
        h_base = p.val("sv_h")
        for i in range(NCH):
            par = i % 2
            w = [("ss", c_routingT)] if i < 2 else [("sv_h", h_base + i - 1)]
            c_mm = p.op("tensor", "matmul", sc_ps[par][:],
                        ics_sb[:, i * 128:(i + 1) * 128], routingT[:],
                        start=True, stop=True, waits=w, inc=("st_h", 1))
            w2 = [("st_h", c_mm)]
            if i % 8 == 0:
                w2.append(("sv", ln_marks[(i // 8) * 8 + 7]))
            if i == 0:
                w2 += list(h_dst_free)
            p.op("vector", "tensor_tensor", XSQ[:, i * TT:(i + 1) * TT],
                 XN[:, i * TT:(i + 1) * TT], sc_ps[par][:], ALU.mult,
                 waits=w2, inc=("sv_h", 1))
        y_base = p.val("sv_y")
        os_base = p.val("ss_os")
        y_group_done = {}
        sgn_ready = {}
        if first_ffn:
            sgn_ready[0] = emit_sign_chunk(0, sgn_free0)
            sgn_ready[1] = emit_sign_chunk(1, sgn_free0)
            sgn_ready[2] = emit_sign_chunk(2, sgn_free0)
        for o in range(NCH):
            par = o % 3
            spar = o % 3
            if first_ffn:
                sgw = [("svg", sgn_ready[o])]
            else:
                if o < 3:
                    wly = list(sgn_free0) + [("dSC0", p.val("dSC0")), ("dSC1", p.val("dSC1"))]
                else:
                    wly = [("st_y", y_group_done[o - 3])]
                c_l = p.dma("sync", SGN[:, spar * D:(spar + 1) * D]
                            .rearrange("p (c f) -> p c f", f=128),
                            sgn_c[o].rearrange("(c p) f -> p c f", p=128),
                            "dY%d" % spar, waits=wly)
                sgw = [("dY%d" % spar, c_l)]
            for i in range(NCH):
                w = list(sgw) if i == 0 else []
                if i == 0 and o == 2:
                    w.append(("ss", c_routingT))
                if o == 0:
                    w.append(("sv_h", h_base + i + 1))
                if i == 0 and o >= 3:
                    w.append(("sv_y", y_base + o - 2))
                inc = ("st_y", 1) if i == NCH - 1 else None
                c_y = p.op("tensor", "matmul", y_ps[par][:],
                           SGN[:, spar * D + i * 128: spar * D + (i + 1) * 128],
                           XSQ[:, i * TT:(i + 1) * TT],
                           start=(i == 0), stop=(i == NCH - 1), waits=w, inc=inc)
            y_group_done[o] = c_y
            w = [("ss_os", os_base + o - 2)] if o >= 3 else []
            if o == 2:
                w.append(("ss", c_routingT))
            c_os_mm = p.op("tensor", "matmul", os_ps[par][:],
                           ocs_sb[:, o * 128:(o + 1) * 128], routingT[:],
                           start=True, stop=True, waits=w, inc=("st_y", 1))
            c_os = p.op("scalar", "activation", osc_sb[:, par * TT:(par + 1) * TT],
                        os_ps[par][:], AF.Copy, waits=[("st_y", c_os_mm)],
                        inc=("ss_os", 1))
            pw = [("st_y", c_os_mm), ("ss_os", c_os)]
            if o % 8 == 0:
                pw.append(("sv", ln_marks[(o // 8) * 8 + 7]))
            p.op("vector", "tensor_tensor", tmpf2[:], y_ps[par][:],
                 osc_sb[:, par * TT:(par + 1) * TT], ALU.mult, waits=pw)
            p.op("vector", "tensor_scalar", tmpf1[:], tmpf2[:],
                 hbias_sb[:, o:o + 1], 0.5, ALU.add, ALU.mult)
            p.op("vector", "tensor_tensor", XF[:, o * TT:(o + 1) * TT],
                 XF[:, o * TT:(o + 1) * TT], tmpf1[:], ALU.add,
                 inc=("sv_y", 1))
            if first_ffn and o + 3 < NCH:
                sgn_ready[o + 3] = emit_sign_chunk(o + 3, [("st_y", y_group_done[o])])
        return p.val("sv_y"), p.val("st_y"), h_base + NCH

    def emit_phase1_tile(tile, consts):
        first = (tile == 0)
        wv = [("dXSP", p.val("dXSP"))] if tile == 1 else []
        ln1_gw = []
        for g in range(4):
            c_xg = p.dma("sync", XF[:, 8 * g * TT:8 * (g + 1) * TT]
                         .rearrange("p (c t) -> p c t", t=TT),
                         xT.rearrange("(c p) t -> p c t", p=128)
                         [:, 8 * g:8 * (g + 1), tile * TT:(tile + 1) * TT],
                         "dXL%d" % g, waits=(wv if g == 0 else []))
            ln1_gw.append([("dXL%d" % g, c_xg)])
            if tile == 1 and g == 0:
                ln1_gw[0].append(("dXS", p.val("dXS")))
                ln1_gw[0].append(("dXSP", p.val("dXSP")))
        ln1 = emit_ln(group_waits=ln1_gw)
        c_rt = emit_router(ln1, consts["c_id"])
        if first:
            sf0 = sf1 = []
        else:
            sf0 = [("st_y", consts["sgn_last_t"])]
            sf1 = [("st_y", consts["sgn_last_t"])]
        c_ffn, st_y_end, h_end = emit_ffn(c_rt, ln1, first, sf0, sf1)
        consts["sgn_last_t"] = st_y_end
        ln2 = emit_ln(group_waits=[
            [("sv_y", c_ffn - 32 + 8 * (g + 1)), ("sv_h", h_end - 32 + 8 * (g + 1))]
            for g in range(4)], sq_waits=[("st_y", st_y_end)])
        for g in range(4):
            p.dma("sync", xsp.rearrange("(c p) t -> p c t", p=128)
                  [:, 8 * g:8 * (g + 1), tile * TT:(tile + 1) * TT],
                  XF[:, 8 * g * TT:8 * (g + 1) * TT].rearrange("p (c t) -> p c t", t=TT),
                  "dXSP", waits=[("sv_y", c_ffn - 32 + 8 * (g + 1))])
        c_ag = p.dma("sync", agx_in[tile].rearrange("(c p) t -> p c t", p=128),
                     XN[:].rearrange("p (c t) -> p c t", t=TT), "dXS",
                     waits=[("sv", ln2[NCH - 1])])
        p.op("gpsimd", "collective_compute", "AllGather", ALU.bypass,
             replica_groups=[list(range(NC))],
             ins=[agx_in[tile]], outs=[agx_out[tile]],
             waits=[("dXS", c_ag)], inc=("ccx%d" % tile, 1))

    def emit_phase3(consts):
        q_ps = [PS[0], PS[1]]
        k_ps = [PS[2], PS[3]]
        v_ps = [PS[4], PS[5]]
        xwin = [XFb[:, 0:NCH * TT], XFb[:, NCH * TT:2 * NCH * TT]]
        wv_sb = XN[:, 0:NCH * 512]
        c_wv = p.dma("gpsimd", wv_sb.rearrange("p (c f) -> p c f", f=512),
                     wv_t.rearrange("(c p) f -> p c f", p=128), "dG",
                     waits=[("dXS", p.val("dXS"))])
        for hp in range(2):
            hsl = slice(hp * 256, (hp + 1) * 256)
            wq_sb = XSQ[:, 0:NCH * 256]
            wk_sb = XSQ[:, NCH * 256:2 * NCH * 256]
            w0 = []
            if hp == 1:
                w0 = [("st3", p.val("st3"))]
            c_wq = p.dma("gpsimd", wq_sb.rearrange("p (c f) -> p c f", f=256),
                         wq_t[:, hsl].rearrange("(c p) f -> p c f", p=128), "dG",
                         waits=w0)
            c_wk = p.dma("gpsimd", wk_sb.rearrange("p (c f) -> p c f", f=256),
                         wk_t[:, hsl].rearrange("(c p) f -> p c f", p=128), "dG")
            widx = 0
            for b in range(2):
                for r in range(NC):
                    par = widx % 2
                    w = [("ccx%d" % b, 1)]
                    st3w_now = p.val("st3w")
                    if hp * 16 + widx >= 2:
                        w.append(("st3w", st3w_now - 1))
                    c_win = p.dma("sync", xwin[par][:].rearrange("p (c t) -> p c t", t=TT),
                                  agx_out[b].rearrange("(r c p) t -> r p c t", p=128, c=NCH)[r],
                                  "dWN%d" % par, waits=w)
                    gtok = r * TPC + b * TT
                    for hl in range(2):
                        h = hp * 2 + hl
                        qp = hl % 2
                        n_qk = p.val("ss3q")
                        w0 = [("dWN%d" % par, c_win)]
                        if hp == 0 and widx == 0 and hl == 0:
                            w0.append(("dG", c_wk))
                        if n_qk >= 2:
                            w0.append(("ss3q", n_qk - 1))
                        for i in range(NCH):
                            w_ = w0 if i == 0 else []
                            inc = ("st3", 1) if i == NCH - 1 else None
                            c_q = p.op("tensor", "matmul", q_ps[qp][:],
                                       wq_sb[:, i * 256 + hl * 128: i * 256 + (hl + 1) * 128],
                                       xwin[par][:, i * TT:(i + 1) * TT],
                                       start=(i == 0), stop=(i == NCH - 1),
                                       waits=w_, inc=inc)
                        n_k = p.val("ss3k")
                        w0 = [("ss3k", n_k - 1)] if n_k >= 2 else []
                        for i in range(NCH):
                            w_ = w0 if i == 0 else []
                            inc = ("st3", 1) if i == NCH - 1 else None
                            c_k = p.op("tensor", "matmul", k_ps[qp][:],
                                       wk_sb[:, i * 256 + hl * 128: i * 256 + (hl + 1) * 128],
                                       xwin[par][:, i * TT:(i + 1) * TT],
                                       start=(i == 0), stop=(i == NCH - 1),
                                       waits=w_, inc=inc)
                        sl = p.val("ss3q") % 2
                        c_eq = p.op("vector", "tensor_scalar", stq[:, sl * 512:(sl + 1) * 512],
                                    q_ps[qp][:], bq_sb[:, h:h + 1], None, ALU.add,
                                    waits=[("st3", c_q), ("dQ%d" % sl, p.val("dQ%d" % sl))],
                                    inc=("ss3q", 1))
                        p.dma("sync", qsp[h][:, gtok:gtok + TT],
                              stq[:, sl * 512:(sl + 1) * 512], "dQ%d" % sl,
                              waits=[("ss3q", c_eq)])
                        sl = p.val("ss3k") % 2
                        c_ek = p.op("vector", "tensor_scalar", stk[:, sl * 512:(sl + 1) * 512],
                                    k_ps[qp][:], bk_sb[:, h:h + 1], None, ALU.add,
                                    waits=[("st3", c_k), ("dK%d" % sl, p.val("dK%d" % sl))],
                                    inc=("ss3k", 1))
                        p.dma("sync", ksp[h][:, gtok:gtok + TT],
                              stk[:, sl * 512:(sl + 1) * 512], "dK%d" % sl,
                              waits=[("ss3k", c_ek)])
                    if hp == 0:
                        for tsl in range(4):
                            vp = tsl % 2
                            n_v = p.val("sv3")
                            w0 = [("dWN%d" % par, c_win)]
                            if widx == 0 and tsl == 0:
                                w0.append(("dG", c_wv))
                            if n_v >= 2:
                                w0.append(("sv3", n_v - 1))
                            for i in range(NCH):
                                w_ = w0 if i == 0 else []
                                inc = ("st3", 1) if i == NCH - 1 else None
                                c_v = p.op("tensor", "matmul", v_ps[vp][:],
                                           xwin[par][:, i * TT + tsl * 128: i * TT + (tsl + 1) * 128],
                                           wv_sb[:, i * 512:(i + 1) * 512],
                                           start=(i == 0), stop=(i == NCH - 1),
                                           waits=w_, inc=inc)
                            sl = p.val("sv3") % 2
                            c_ev = p.op("vector", "tensor_tensor", stv[:, sl * 512:(sl + 1) * 512],
                                        v_ps[vp][:], bvb_sb[:], ALU.add,
                                        waits=[("st3", c_v), ("dV%d" % sl, p.val("dV%d" % sl))],
                                        inc=("sv3", 1))
                            gt = gtok + tsl * 128
                            for hh in range(HPC):
                                p.dma("sync", vsp[hh][gt:gt + 128, :],
                                      stv[:, sl * 512 + hh * 128: sl * 512 + (hh + 1) * 128],
                                      "dV%d" % sl, waits=[("sv3", c_ev)] if hh == 0 else [])
                    p.op("tensor", "nop", inc=("st3w", 1))
                    widx += 1
        return {"dQ": (p.val("dQ0"), p.val("dQ1")),
                "dK": (p.val("dK0"), p.val("dK1")),
                "dV": (p.val("dV0"), p.val("dV1"))}

    def emit_phase4(p3, consts):
        slots = {
            0: {"K": XFb[:, 0:8192], "Q": XFb[:, 8192:16384], "V": XFb[:, 16384:24576]},
            1: {"K": XSQ[:, 0:8192], "Q": XSQ[:, 8192:16384], "V": XFb[:, 24576:32768]},
        }
        qsq = XN[:, 0:8192]
        ppp = [XN[:, 8192 + j * 1024: 8192 + (j + 1) * 1024] for j in range(3)]
        sta = [stq[:, 0:512], stq[:, 512:1024]]
        sp_ps = [PSQ[0], PSQ[1], PSQ[2]]
        o_ps = PS[6]
        dF = PS[7]
        f_ps = PS[7]
        n_ps2 = [PSQ[0][0:1, 0:512], PSQ[0][0:1, 512:1024]]

        head_done = {}
        hload = {}

        def load_head(h):
            par = h % 2
            if h < 2:
                w = [("dQ0", p3["dQ"][0]), ("dQ1", p3["dQ"][1]),
                     ("dK0", p3["dK"][0]), ("dK1", p3["dK"][1]),
                     ("dV0", p3["dV"][0]), ("dV1", p3["dV"][1])]
            else:
                w = [("st4", head_done[h - 2])]
            p.dma("sync", slots[par]["K"], ksp[h], "dH%d" % par, waits=w)
            p.dma("sync", slots[par]["Q"], qsp[h], "dH%d" % par)
            p.dma("sync", slots[par]["V"].rearrange("p (tt d) -> p tt d", d=128),
                  vsp[h].rearrange("(tt p) d -> p tt d", p=128), "dH%d" % par)
            return p.val("dH%d" % par)

        hload[0] = load_head(0)
        hload[1] = load_head(1)
        sp_led = [0, 0, 0]
        pp_led = [0, 0, 0]
        f_led = 0
        t1_led = 0
        t2_led = [0]
        o_led = ("sv4", 0)
        d_led = 0
        n_led = [0, 0]
        for h in range(HPC):
            par = h % 2
            sl = slots[par]
            for qk in range(2):
                src = sl["Q"] if qk == 0 else sl["K"]
                w = [("dH%d" % par, hload[h]), ("st4n", p.val("st4n"))]
                c_sq2 = p.op("scalar", "activation", qsq, src, AF.Square,
                             waits=w, inc=("ss4", 1))
                for w16 in range(16):
                    nb = w16 % 2
                    ww = [("ss4", c_sq2)] if w16 == 0 else []
                    ww.append(("sv4", n_led[nb]))
                    if w16 == 0 and qk == 0:
                        ww.append(("ss4", sp_led[0]))
                    c_nm = p.op("tensor", "matmul", n_ps2[nb], ones_col[:],
                                qsq[:, w16 * 512:(w16 + 1) * 512],
                                start=True, stop=True,
                                waits=ww, inc=("st4n", 1))
                    n_led[nb] = p.op("vector", "reduce_max",
                                     rowsB[:, 1024 + w16:1025 + w16],
                                     n_ps2[nb], AX.X, waits=[("st4n", c_nm)],
                                     inc=("sv4", 1))
                p.wait("vector", ("sv4", max(n_led)))
                c_mx4 = p.op("vector", "reduce_max", rowsB[:, 1040 + qk:1041 + qk],
                             rowsB[:, 1024:1040], AX.X, inc=("sv4", 1))
                p.wait("vector", ("sv4", c_mx4))
                if _DEV_STAGE == "attn_dbg":
                    p.dma("sync", dbg_nrm[h][qk:qk + 1, 0:20], rowsB[:, 1024:1044],
                          "dDBG", waits=[("sv4", c_mx4)])
                    if qk == 0:
                        p.dma("sync", dbg_qsq[h], qsq[:, 0:512], "dDBG",
                              waits=[("sv4", c_mx4)])
                        p.dma("sync", dbg_qslot[h], sl["Q"][:, 0:512], "dDBG",
                              waits=[("sv4", c_mx4)])
            c_pr = p.op("vector", "tensor_mul", rowsB[:, 1042:1043],
                        rowsB[:, 1040:1041], rowsB[:, 1041:1042], inc=("sv4", 1))
            p.wait("vector", ("sv4", c_pr))
            c_lnn = p.op("scalar", "activation", rowsB[:, 1042:1043],
                         rowsB[:, 1042:1043], AF.Ln, waits=[("sv4", c_pr)],
                         inc=("ss4", 1))
            p.wait("scalar", ("ss4", c_lnn))
            c_sqr = p.op("scalar", "activation", rowsB[:, 1042:1043],
                         rowsB[:, 1042:1043], AF.Exp, scale=0.5, inc=("ss4", 1))
            c_nc = p.op("vector", "tensor_scalar", rowsB[:, 1042:1043],
                        rowsB[:, 1042:1043], -ISQ, -1.0, ALU.mult, ALU.add,
                        waits=[("ss4", c_sqr)], inc=("sv4", 1))
            c_bcc = p.op("tensor", "matmul", f_ps[:, 0:1], ones_row_f[:],
                         rowsB[:, 1042:1043], start=True, stop=True,
                         waits=[("sv4", c_nc), ("ss4f", f_led), ("sv4", d_led)],
                         inc=("st4f", 1))
            c_negc = p.op("scalar", "activation", negC[:], f_ps[:, 0:1], AF.Copy,
                          waits=[("st4f", c_bcc)], inc=("ss4f", 1))
            f_led = c_negc
            p.wait("scalar", ("ss4f", c_negc))
            if _DEV_STAGE == "attn_dbg":
                p.dma("sync", dbg_nc[h], negC[0:1, 0:1], "dDBG",
                      waits=[("ss4f", c_negc)])
            if h >= 1:
                p.wait("vector", ("dA%d_0" % (h - 1), p.val("dA%d_0" % (h - 1))),
                       ("dA%d_1" % (h - 1), p.val("dA%d_1" % (h - 1))))
            for qb in range(16):
                nkt = 4 * (qb + 1)
                npair = nkt // 2
                smm = {}
                avw = {}

                def emit_spair(j):
                    pb = j % 3
                    w = [("ss4", sp_led[pb])]
                    if j == 0:
                        w.append(("dH%d" % par, hload[h]))
                        if qb == 0:
                            w.append(("sv4", max(n_led)))
                    p.op("tensor", "matmul", sp_ps[pb][:, 0:512],
                         sl["K"][:, (2 * j) * 128:(2 * j + 1) * 128],
                         sl["Q"][:, qb * 512:(qb + 1) * 512],
                         start=True, stop=True, waits=w)
                    smm[j] = p.op("tensor", "matmul", sp_ps[pb][:, 512:1024],
                                  sl["K"][:, (2 * j + 1) * 128:(2 * j + 2) * 128],
                                  sl["Q"][:, qb * 512:(qb + 1) * 512],
                                  start=True, stop=True, inc=("st4s", 1))

                def emit_exppair(j):
                    pb = j % 3
                    w = [("st4s", smm[j]), ("st4", pp_led[pb]), ("ss4f", c_negc)]
                    c_e = p.op("scalar", "activation", ppp[pb][:],
                               sp_ps[pb][:], AF.Exp, bias=negC[:, 0:1],
                               scale=ISQ, waits=w, inc=("ss4", 1))
                    sp_led[pb] = c_e
                    for hh in range(2):
                        kt = 2 * j + hh
                        if kt >= 4 * qb:
                            c_m = p.op("gpsimd", "affine_select",
                                       ppp[pb][:, hh * 512:(hh + 1) * 512],
                                       ppp[pb][:, hh * 512:(hh + 1) * 512],
                                       [[1, 512]], ALU.is_ge, 0.0,
                                       base=qb * 512 - kt * 128,
                                       channel_multiplier=-1,
                                       waits=[("ss4", c_e)], inc=("sg4", 1))
                            avw[kt] = ("sg4", c_m)
                        else:
                            avw[kt] = ("ss4", c_e)

                emit_spair(0)
                emit_exppair(0)
                if npair > 1:
                    emit_spair(1)
                    emit_exppair(1)
                c_den = None
                for j in range(npair):
                    pb = j % 3
                    if j + 2 < npair:
                        emit_spair(j + 2)
                        emit_exppair(j + 2)
                    for hh in range(2):
                        kt = 2 * j + hh
                        pphalf = ppp[pb][:, hh * 512:(hh + 1) * 512]
                        w = [avw[kt]]
                        if kt == 0:
                            w.append(o_led)
                        p.op("tensor", "matmul", o_ps[:],
                             sl["V"][:, kt * 128:(kt + 1) * 128],
                             pphalf, start=(kt == 0), stop=(kt == nkt - 1), waits=w)
                        w = []
                        if kt == 0:
                            w = [("sv4", d_led), ("ss4f", f_led)]
                        c_den = p.op("tensor", "matmul", dF[:], onesM[:],
                                     pphalf,
                                     start=(kt == 0), stop=(kt == nkt - 1),
                                     waits=w, inc=("st4", 1))
                    pp_led[pb] = c_den
                c_oev = p.op("scalar", "activation", tmpf2[:], o_ps[:], AF.Copy,
                             waits=[("st4", c_den), ("sv4", t2_led[0])],
                             inc=("ss4f", 1))
                o_led = ("ss4f", c_oev)
                c_rc = p.op("vector", "reciprocal", rowsA[:, 1024:1536],
                            dF[0:1, :], waits=[("st4", c_den)], inc=("sv4", 1))
                d_led = c_rc
                if _DEV_STAGE == "attn_dbg":
                    p.dma("sync", dbg_den[h][qb:qb + 1, :], rowsA[:, 1024:1536],
                          "dDBG", waits=[("sv4", c_rc)])
                c_fb = p.op("tensor", "matmul", f_ps[:], ones_row_f[:],
                            rowsA[:, 1024:1536], start=True, stop=True,
                            waits=[("sv4", c_rc), ("ss4f", f_led)], inc=("st4f", 1))
                c_cp = p.op("scalar", "activation", tmpf1[:], f_ps[:], AF.Copy,
                            waits=[("st4f", c_fb), ("sv4", t1_led)], inc=("ss4f", 1))
                f_led = c_cp
                ssl = qb % 2
                sA = "dA%d_%d" % (h, ssl)
                c_tt = p.op("vector", "tensor_tensor", sta[ssl], tmpf2[:], tmpf1[:],
                            ALU.mult,
                            waits=[("ss4f", c_cp), (sA, p.val(sA))],
                            inc=("sv4", 1))
                t1_led = c_tt
                t2_led[0] = c_tt
                p.dma("sync", aga_in[h * 128:(h + 1) * 128, qb * 512:(qb + 1) * 512],
                      sta[ssl], sA, waits=[("sv4", c_tt)])
            head_done[h] = p.val("st4")
            if h + 2 < HPC:
                hload[h + 2] = load_head(h + 2)
            p.op("gpsimd", "collective_compute", "AllGather", ALU.bypass,
                 replica_groups=[list(range(NC))],
                 ins=[aga_in[h * 128:(h + 1) * 128, :]], outs=[aga_out[h]],
                 waits=[("dA%d_0" % h, p.val("dA%d_0" % h)),
                        ("dA%d_1" % h, p.val("dA%d_1" % h))],
                 inc=("cca%d" % h, 1))

    def emit_phase6_tile(tile, consts):
        w = []
        if tile == 1:
            w = [("dOUT", consts["c_out0"])]
        c_x = p.dma("sync", XF[:].rearrange("p (c t) -> p c t", t=TT),
                    xsp.rearrange("(c p) t -> p c t", p=128)[:, :, tile * TT:(tile + 1) * TT],
                    "dXL2", waits=w)
        ww = [("cca%d" % hh, 1) for hh in range(HPC)]
        if tile == 1:
            ww = [("st6", consts["st6_0"]), ("st", p.val("st")),
                  ("st_y", consts["sgn_last_t"])]
        xsq_v = XSQ[:].rearrange("p (r four t) -> p r four t", four=HPC, t=TT)
        for hl in range(HPC):
            w_ = ww if hl == 0 else []
            wl = [(p.sem(n), v) for (n, v) in w_ if v > 0]
            dst = xsq_v[:, :, hl, :]
            sem = p.sem("dAT")
            p.inc("dAT", 16)

            def f(e, wl=wl, dst=dst, hl=hl, tile=tile, sem=sem):
                for s_, v_ in wl:
                    e.wait_ge(s_, v_)
                if ("idx%d" % tile) not in p.state:
                    p.state["idx%d" % tile] = e.snap(p.state["cid"] * 2 + tile)
                idx = p.state["idx%d" % tile]
                src = aga_out[hl].rearrange("(r p) s -> p r s", p=128)[:, :, bass.ts(idx, TT)]
                e.dma_start(out=dst, in_=src).then_inc(sem, 16)
            p.raw("sync", f)
        c_at = p.val("dAT")
        y_ps = [PS[0], PS[1], PS[2]]
        ob = p.val("sv_o")
        st6_base = p.val("st6")
        y6_done = {}
        for o in range(NCH):
            par = o % 3
            spar = o % 3
            if o < 3:
                wol = list(consts["sgn_free6"][min(o, len(consts["sgn_free6"]) - 1)])
            else:
                wol = [("st6", y6_done[o - 3])]
            c_wo = p.dma("gpsimd", SGN[:, spar * D:(spar + 1) * D]
                         .rearrange("p (c f) -> p c f", f=128),
                         wo_tc[o].rearrange("(c p) f -> p c f", p=128),
                         "dWO%d" % spar, waits=wol)
            for i in range(NCH):
                w_ = [("dWO%d" % spar, c_wo), ("dAT", c_at)] if i == 0 else []
                if i == 0 and o >= 3:
                    w_.append(("sv_o", ob + o - 2))
                inc = ("st6", 1) if i == NCH - 1 else None
                c_y = p.op("tensor", "matmul", y_ps[par][:],
                           SGN[:, spar * D + i * 128: spar * D + (i + 1) * 128],
                           XSQ[:, i * TT:(i + 1) * TT],
                           start=(i == 0), stop=(i == NCH - 1), waits=w_, inc=inc)
            y6_done[o] = c_y
            w_ = [("st6", c_y)]
            if o == 0:
                w_.append(("dXL2", c_x))
            p.op("vector", "scalar_tensor_tensor", XF[:, o * TT:(o + 1) * TT],
                 y_ps[par][:], bo_sb[:, o:o + 1], XF[:, o * TT:(o + 1) * TT],
                 ALU.add, ALU.add, waits=w_, inc=("sv_o", 1))
        consts["st6_0"] = p.val("st6")
        if _DEV_STAGE in ("attn", "attn_dbg"):
            c_out = p.dma("sync", yT.rearrange("(c p) t -> p c t", p=128)[:, :, tile * TT:(tile + 1) * TT],
                          XF[:].rearrange("p (c t) -> p c t", t=TT), "dOUT",
                          waits=[("sv_o", p.val("sv_o"))])
            consts["c_out0"] = c_out
            consts["sgn_free6"] = [[("st6", y6_done[NCH - 2])], [("st6", y6_done[NCH - 1])]]
            consts["sgn_last_t"] = p.val("st_y")
            return c_out
        svo = p.val("sv_o")
        ln3 = emit_ln(group_waits=[[("sv_o", svo - 32 + 8 * (g + 1))] for g in range(4)],
                      sq_waits=[("st6", y6_done[NCH - 1])])
        if _DEV_STAGE == "full_dbg":
            p.dma("sync", dbg_x3[tile], XF[:], "dDBG", waits=[("sv", ln3[NCH - 1])])
            p.dma("sync", dbg_xn3[tile], XN[:], "dDBG", waits=[("sv", ln3[NCH - 1])])
        c_rt = emit_router(ln3, consts["c_id"])
        if _DEV_STAGE == "full_dbg":
            p.dma("sync", dbg_rt[tile], routingT[:], "dDBG", waits=[("ss", c_rt)])
        sf0 = [("st6", y6_done[NCH - 1])]
        sf1 = sf0
        c_ffn, st_y_end, _h6 = emit_ffn(c_rt, ln3, False, sf0, sf1,
                                        h_dst_free=[("st6", y6_done[NCH - 1])])
        consts["sgn_last_t"] = st_y_end
        consts["sgn_free6"] = [[("st_y", st_y_end)]] * 3
        c_out = p.dma("sync", yT.rearrange("(c p) t -> p c t", p=128)[:, :, tile * TT:(tile + 1) * TT],
                      XF[:].rearrange("p (c t) -> p c t", t=TT), "dOUT",
                      waits=[("sv_y", c_ffn)])
        consts["c_out0"] = c_out
        return c_out

    with es:
        with nc.Block() as block:
            consts = phase0()
            p.fence()
            emit_phase1_tile(0, consts)
            emit_phase1_tile(1, consts)
            if _DEV_STAGE == "ffn1":
                c_o = p.dma("sync", yT.rearrange("(c p) t -> p c t", p=128),
                            xsp.rearrange("(c p) t -> p c t", p=128), "dOUT",
                            waits=[("dXS", p.val("dXS"))])
                p.wait("sync", ("dOUT", c_o))
            else:
                p.fence()
                p3 = emit_phase3(consts)
                p.fence()
                emit_phase4(p3, consts)
                p.fence()
                if _DEV_STAGE == "attn_dbg":
                    p.dma("sync", dbg_q[:, :, :], qsp[:, :, :], "dDBG")
                    p.dma("sync", dbg_k[:, :, :], ksp[:, :, :], "dDBG")
                    p.dma("sync", dbg_v[:, :, :], vsp[:, :, :], "dDBG")
                    p.dma("sync", dbg_ain[:, :], aga_in[:, :], "dDBG")
                    p.dma("sync", dbg_aout[:, :, :], aga_out[:, :, :], "dDBG")
                    p.dma("sync", dbg_xn2[:, 0:TT], agx_out[0][:, :], "dDBG")
                    c_dbg = p.dma("sync", dbg_xn2[:, TT:2 * TT], agx_out[1][:, :], "dDBG")
                    p.wait("sync", ("dDBG", c_dbg))
                consts["sgn_free6"] = [[], [], []]
                emit_phase6_tile(0, consts)
                emit_phase6_tile(1, consts)
                p.wait("sync", ("dOUT", p.val("dOUT")))

            for eng_name in ENGS:
                def make_body(name):
                    def body(e):
                        for f in p.streams[name]:
                            f(e)
                    return body
                getattr(block, eng_name)(make_body(eng_name))
    return nc


def kernel(L0_attn_x, L0_attn_Wq, L0_attn_bq, L0_attn_Wk, L0_attn_bk,
           L0_attn_Wv, L0_attn_bv, L0_attn_Wo, L0_attn_bo,
           L0_ffn_weight, L0_ffn_bias, L0_ffn_gate_w,
           L0_ffn_in_channel_scale, L0_ffn_out_channel_scale):
    x = np.asarray(L0_attn_x, np.float32).reshape(S, D)
    xTfull = np.ascontiguousarray(x.T)
    wq_T = np.ascontiguousarray(np.asarray(L0_attn_Wq, np.float32).T)
    wk_T = np.ascontiguousarray(np.asarray(L0_attn_Wk, np.float32).T)
    wv_T = np.ascontiguousarray(np.asarray(L0_attn_Wv, np.float32).T)
    wo_T = np.asarray(L0_attn_Wo, np.float32).T
    wf_T = np.asarray(L0_ffn_weight, np.float32).T
    wo_tc = np.ascontiguousarray(wo_T.reshape(D, NCH, 128).transpose(1, 0, 2))
    wf_tc = np.ascontiguousarray(wf_T.reshape(D, NCH, 128).transpose(1, 0, 2))
    gate_T = np.ascontiguousarray(np.asarray(L0_ffn_gate_w, np.float32).T)
    ics = np.ascontiguousarray(np.asarray(L0_ffn_in_channel_scale, np.float32))
    ocs = np.ascontiguousarray(np.asarray(L0_ffn_out_channel_scale, np.float32))
    bq = np.asarray(L0_attn_bq, np.float32)
    bk = np.asarray(L0_attn_bk, np.float32)
    bv = np.asarray(L0_attn_bv, np.float32)
    bo = np.asarray(L0_attn_bo, np.float32)
    bfb = np.asarray(L0_ffn_bias, np.float32)
    bo_f = np.ascontiguousarray(bo.reshape(NCH, 128).T)
    bias_f = np.ascontiguousarray(bfb.reshape(NCH, 128).T)

    in_maps = []
    for r in range(NC):
        hs = slice(r * HPC * HD, (r + 1) * HPC * HD)
        in_maps.append({
            "xT": np.ascontiguousarray(xTfull[:, r * TPC:(r + 1) * TPC]),
            "wq_t": np.ascontiguousarray(wq_T[:, hs]),
            "wk_t": np.ascontiguousarray(wk_T[:, hs]),
            "wv_t": np.ascontiguousarray(wv_T[:, hs]),
            "wo_tc": wo_tc,
            "wf_tc": wf_tc,
            "gate_t": gate_T,
            "icsM": ics,
            "ocsM": ocs,
            "bq_f": np.ascontiguousarray(bq[hs].reshape(HPC, 128).T),
            "bk_f": np.ascontiguousarray(bk[hs].reshape(HPC, 128).T),
            "bv_r": np.ascontiguousarray(bv[hs].reshape(1, HPC * HD)),
            "bo_f": bo_f,
            "bias_f": bias_f,
            "cid": np.array([[r]], dtype=np.int32),
        })
    nc = build_program()
    res = run_bass_kernel_spmd(nc, in_maps, core_ids=list(range(NC)))
    out = np.empty((S, D), np.float32)
    for r in range(NC):
        out[r * TPC:(r + 1) * TPC, :] = np.asarray(res.results[r]["yT"]).T
    return out.reshape(1, S, D)


# version 17
# speedup vs baseline: 1.1327x; 1.1327x over previous
import math
import numpy as np
from contextlib import ExitStack

import concourse.bass as bass
import concourse.mybir as mybir
from concourse.bass_utils import run_bass_kernel_spmd

F32 = mybir.dt.float32
BF16 = mybir.dt.bfloat16
I32 = mybir.dt.int32
AF = mybir.ActivationFunctionType
ALU = mybir.AluOpType
AX = mybir.AxisListType

D = 4096
S = 8192
NC = 8
TPC = S // NC
TT = 512
HPC = 4
HD = 128
NCH = D // 128
ISQ = 1.0 / math.sqrt(HD)
EPS = 1e-5

_DEV_STAGE = None

ENGS = ("sync", "tensor", "vector", "scalar", "gpsimd")


class P:
    def __init__(self, nc, es):
        self.nc = nc
        self.es = es
        self.streams = {n: [] for n in ENGS}
        self._sems = {}
        self.cnt = {}
        self.state = {}

    def sem(self, name):
        if name not in self._sems:
            self._sems[name] = self.es.enter_context(self.nc.semaphore(name))
            self.cnt[name] = 0
        return self._sems[name]

    def inc(self, name, k=1):
        self.sem(name)
        self.cnt[name] += k
        return self.cnt[name]

    def val(self, name):
        self.sem(name)
        return self.cnt[name]

    def op(self, eng, meth, *args, waits=(), inc=None, **kw):
        wl = [(self.sem(n), v) for (n, v) in waits if v > 0]
        after = None
        sem_i = None
        k_i = 0
        if inc is not None:
            after = self.inc(inc[0], inc[1])
            sem_i = self._sems[inc[0]]
            k_i = inc[1]

        def f(e, meth=meth, args=args, kw=kw, wl=wl, sem_i=sem_i, k_i=k_i):
            for s_, v_ in wl:
                e.wait_ge(s_, v_)
            ins = getattr(e, meth)(*args, **kw)
            if sem_i is not None:
                ins.then_inc(sem_i, k_i)
        self.streams[eng].append(f)
        return after

    def raw(self, eng, fn):
        self.streams[eng].append(fn)

    def wait(self, eng, *waits):
        wl = [(self.sem(n), v) for (n, v) in waits if v > 0]

        def f(e, wl=wl):
            for s_, v_ in wl:
                e.wait_ge(s_, v_)
        self.streams[eng].append(f)

    def dma(self, eng, out, in_, sem, waits=(), **kw):
        return self.op(eng, "dma_start", waits=waits, inc=(sem, 16), out=out, in_=in_, **kw)

    def fence(self):
        name = "fence"
        self.sem(name)
        base = self.cnt[name]
        for eng in ENGS:
            meth = "engine_nop" if eng in ("vector", "gpsimd") else "nop"
            self.op(eng, meth, inc=(name, 1))
        target = base + len(ENGS)
        for eng in ENGS:
            self.wait(eng, (name, target))
        return target


def build_program():
    nc = bass.Bass("TRN2", target_bir_lowering=False)

    xT = nc.dram_tensor("xT", [D, TPC], F32, kind="ExternalInput").ap()
    wq_t = nc.dram_tensor("wq_t", [D, HPC * HD], F32, kind="ExternalInput").ap()
    wk_t = nc.dram_tensor("wk_t", [D, HPC * HD], F32, kind="ExternalInput").ap()
    wv_t = nc.dram_tensor("wv_t", [D, HPC * HD], F32, kind="ExternalInput").ap()
    wo_tc = nc.dram_tensor("wo_tc", [NCH, D, 128], F32, kind="ExternalInput").ap()
    wf_tc = nc.dram_tensor("wf_tc", [NCH, D, 128], F32, kind="ExternalInput").ap()
    gate_t = nc.dram_tensor("gate_t", [D, 8], F32, kind="ExternalInput").ap()
    icsM = nc.dram_tensor("icsM", [8, D], F32, kind="ExternalInput").ap()
    ocsM = nc.dram_tensor("ocsM", [8, D], F32, kind="ExternalInput").ap()
    bq_f = nc.dram_tensor("bq_f", [128, HPC], F32, kind="ExternalInput").ap()
    bk_f = nc.dram_tensor("bk_f", [128, HPC], F32, kind="ExternalInput").ap()
    bv_r = nc.dram_tensor("bv_r", [1, HPC * HD], F32, kind="ExternalInput").ap()
    bo_f = nc.dram_tensor("bo_f", [128, NCH], F32, kind="ExternalInput").ap()
    bias_f = nc.dram_tensor("bias_f", [128, NCH], F32, kind="ExternalInput").ap()
    cid = nc.dram_tensor("cid", [1, 1], I32, kind="ExternalInput").ap()
    yT = nc.dram_tensor("yT", [D, TPC], F32, kind="ExternalOutput").ap()
    if _DEV_STAGE == "full_dbg":
        dbg_xn3 = nc.dram_tensor("dbg_xn3", [2, 128, NCH * TT], BF16, kind="ExternalOutput").ap()
        dbg_rt = nc.dram_tensor("dbg_rt", [2, 8, TT], BF16, kind="ExternalOutput").ap()
        dbg_x3 = nc.dram_tensor("dbg_x3", [2, 128, NCH * TT], F32, kind="ExternalOutput").ap()
    if _DEV_STAGE == "attn_dbg":
        dbg_nc = nc.dram_tensor("dbg_nc", [HPC, 1], F32, kind="ExternalOutput").ap()
        dbg_nrm = nc.dram_tensor("dbg_nrm", [HPC, 2, 20], F32, kind="ExternalOutput").ap()
        dbg_qsq = nc.dram_tensor("dbg_qsq", [HPC, HD, 512], BF16, kind="ExternalOutput").ap()
        dbg_qslot = nc.dram_tensor("dbg_qslot", [HPC, HD, 512], BF16, kind="ExternalOutput").ap()
        dbg_den = nc.dram_tensor("dbg_den", [HPC, 16, 512], F32, kind="ExternalOutput").ap()
        dbg_q = nc.dram_tensor("dbg_q", [HPC, HD, S], BF16, kind="ExternalOutput").ap()
        dbg_k = nc.dram_tensor("dbg_k", [HPC, HD, S], BF16, kind="ExternalOutput").ap()
        dbg_v = nc.dram_tensor("dbg_v", [HPC, S, HD], BF16, kind="ExternalOutput").ap()
        dbg_ain = nc.dram_tensor("dbg_ain", [HPC * HD, S], BF16, kind="ExternalOutput").ap()
        dbg_aout = nc.dram_tensor("dbg_aout", [HPC, NC * HD, S], BF16, kind="ExternalOutput").ap()
        dbg_xn2 = nc.dram_tensor("dbg_xn2", [NC * D, TPC], BF16, kind="ExternalOutput").ap()

    sgn_c = nc.dram_tensor("sgn_c", [NCH, D, 128], BF16).ap()
    xsp = nc.dram_tensor("xsp", [D, TPC], F32).ap()
    agx_in = [nc.dram_tensor("agx_in%d" % b, [D, TT], BF16).ap() for b in range(2)]
    agx_out = [nc.dram_tensor("agx_out%d" % b, [NC * D, TT], BF16).ap() for b in range(2)]
    qsp = nc.dram_tensor("qsp", [HPC, HD, S], BF16).ap()
    ksp = nc.dram_tensor("ksp", [HPC, HD, S], BF16).ap()
    vsp = nc.dram_tensor("vsp", [HPC, S, HD], BF16).ap()
    aga_in = nc.dram_tensor("aga_in", [HPC * HD, S], BF16).ap()
    aga_out = nc.dram_tensor("aga_out", [HPC, NC * HD, S], BF16).ap()

    es = ExitStack()
    p = P(nc, es)

    sb = nc.alloc_sbuf_tensor
    ones_col = sb("ones_col", [128, 1], BF16)
    ones_row_f = sb("ones_row_f", [1, 128], F32)
    ident = sb("ident", [128, 128], BF16)
    onesM = sb("onesM", [128, 128], BF16)
    gate_sb = sb("gate_sb", [128, NCH * 8], BF16)
    ics_sb = sb("ics_sb", [8, D], BF16)
    ocs_sb = sb("ocs_sb", [8, D], BF16)
    bq_sb = sb("bq_sb", [128, HPC], F32)
    bk_sb = sb("bk_sb", [128, HPC], F32)
    bo_sb = sb("bo_sb", [128, NCH], F32)
    hbias_sb = sb("hbias_sb", [128, NCH], F32)
    bv_row = sb("bv_row", [1, HPC * HD], F32)
    bvb_sb = sb("bvb_sb", [128, HPC * HD], BF16)
    cid_sb = sb("cid_sb", [1, 1], I32)
    rowsA = sb("rowsA", [1, 1536], F32)
    rowsB = sb("rowsB", [1, 1536], F32)
    routing = sb("routing", [128, 32], BF16)
    routingT = sb("routingT", [8, TT], BF16)
    rsc1 = sb("rsc1", [128, 32], F32)
    rsc2 = sb("rsc2", [128, 32], F32)
    rsc3 = sb("rsc3", [128, 8], F32)
    tmpf1 = sb("tmpf1", [128, 512], F32)
    tmpf2 = sb("tmpf2", [128, 512], F32)
    osc_sb = sb("osc_sb", [128, 3 * 512], BF16)
    negC = sb("negC", [128, 1], F32)
    stq = sb("stq", [128, 2 * 512], BF16)
    stk = sb("stk", [128, 2 * 512], BF16)
    stv = sb("stv", [128, 2 * 512], BF16)

    XF = sb("XF", [128, NCH * TT], F32)
    XN = sb("XN", [128, NCH * TT], BF16)
    XSQ = sb("XSQ", [128, NCH * TT], BF16)
    WST = sb("WST", [128, 2048], F32)
    SGN = sb("SGN", [128, 3 * D], BF16)

    XFb = XF[:].bitcast(BF16)
    WSTb = WST[:].bitcast(BF16)

    PSQ = [nc.alloc_psum_tensor("psq%d" % i, [128, 1024], F32) for i in range(4)]
    PS = [PSQ[i // 2][:, (i % 2) * 512:(i % 2 + 1) * 512] for i in range(8)]

    def phase0():
        p.dma("sync", bq_sb[:], bq_f, "dC")
        p.dma("sync", bk_sb[:], bk_f, "dC")
        p.dma("sync", bo_sb[:], bo_f, "dC")
        p.dma("sync", hbias_sb[:], bias_f, "dC")
        p.dma("sync", cid_sb[:], cid, "dC")
        c_dc = p.dma("sync", bv_row[:], bv_r, "dC")
        p.dma("gpsimd", gate_sb[:].rearrange("p (c e) -> p c e", e=8),
              gate_t.rearrange("(c p) e -> p c e", p=128), "dG")
        p.dma("gpsimd", ics_sb[:], icsM, "dG")
        c_dg = p.dma("gpsimd", ocs_sb[:], ocsM, "dG")
        p.op("vector", "memset", ones_col[:], 1.0)
        c_ones = p.op("vector", "memset", ones_row_f[:], 1.0, inc=("sv", 1))
        p.op("gpsimd", "memset", ident[:], 0.0)
        p.op("gpsimd", "affine_select", ident[:], ident[:], [[-1, 128]],
             ALU.not_equal, 1.0, base=0, channel_multiplier=1)
        p.op("gpsimd", "memset", onesM[:], 1.0)
        c_id = p.op("gpsimd", "affine_select", onesM[:], onesM[:], [[-1, 128]],
                    ALU.is_ge, 0.0, base=0, channel_multiplier=0, inc=("sg", 1))
        p.op("scalar", "nop", waits=[("dC", c_dc)], inc=("ss", 1))
        bc = PS[0][:, 0:HPC * HD]
        c_mm = p.op("tensor", "matmul", bc, ones_row_f[:], bv_row[:],
                    start=True, stop=True,
                    waits=[("dC", c_dc), ("sv", c_ones)], inc=("st", 1))
        p.op("scalar", "activation", bvb_sb[:], bc, AF.Copy,
             waits=[("st", c_mm)], inc=("ss", 1))
        p.wait("sync", ("dC", c_dc))

        def f(e):
            p.state["cid"] = e.value_load(cid_sb[0:1, 0:1])
        p.raw("sync", f)
        p.wait("gpsimd", ("dG", c_dg))
        return {"c_id": c_id}

    def emit_ln(first_waits_v=(), group_waits=None, sq_waits=()):
        r1 = PS[2][0:1, :]
        r2 = PS[3][0:1, :]
        bca = PS[4]
        bcb = PS[5]
        gw = group_waits or [list(first_waits_v)] + [[]] * 3
        c_cast = {}
        c_sq = {}
        for g in range(4):
            cs = 8 * g * TT
            ce = 8 * (g + 1) * TT
            c_cast[g] = p.op("vector", "tensor_copy", XN[:, cs:ce], XF[:, cs:ce],
                             waits=list(gw[g]), inc=("sv", 1))
            wsq = [("sv", c_cast[g])] + (list(sq_waits) if g == 0 else [])
            c_sq[g] = p.op("scalar", "activation", XSQ[:, cs:ce], XN[:, cs:ce],
                           AF.Square, waits=wsq, inc=("ss", 1))
        for c in range(NCH):
            w = [("sv", c_cast[c // 8])] if c % 8 == 0 else []
            p.op("tensor", "matmul", r1, ones_col[:], XN[:, c * TT:(c + 1) * TT],
                 start=(c == 0), stop=(c == NCH - 1), waits=w)
        for c in range(NCH):
            w = [("ss", c_sq[c // 8])] if c % 8 == 0 else []
            inc = ("st", 1) if c == NCH - 1 else None
            c_stats = p.op("tensor", "matmul", r2, ones_col[:], XSQ[:, c * TT:(c + 1) * TT],
                           start=(c == 0), stop=(c == NCH - 1), waits=w, inc=inc)
        mu = rowsA[:, 0:TT]
        mu = rowsA[:, 0:TT]
        m2 = rowsA[:, TT:2 * TT]
        var = rowsB[:, 0:TT]
        a_row = rowsB[:, TT:2 * TT]
        p.op("vector", "tensor_scalar_mul", mu, r1, 1.0 / D, waits=[("st", c_stats)])
        c0 = p.op("vector", "tensor_scalar_mul", m2, r2, 1.0 / D, inc=("sv", 1))
        p.wait("vector", ("sv", c0))
        c0 = p.op("vector", "tensor_mul", var, mu, mu, inc=("sv", 1))
        p.wait("vector", ("sv", c0))
        c0 = p.op("vector", "tensor_sub", var, m2, var, inc=("sv", 1))
        p.wait("vector", ("sv", c0))
        c_var = p.op("vector", "tensor_scalar_add", var, var, EPS, inc=("sv", 1))
        c_ln1 = p.op("scalar", "activation", var, var, AF.Ln,
                     waits=[("sv", c_var)], inc=("ss", 1))
        p.wait("scalar", ("ss", c_ln1))
        c_a = p.op("scalar", "activation", a_row, var, AF.Exp, scale=-0.5,
                   inc=("ss", 1))
        c0 = p.op("vector", "tensor_mul", mu, mu, a_row, waits=[("ss", c_a)],
                  inc=("sv", 1))
        p.wait("vector", ("sv", c0))
        c_b = p.op("vector", "tensor_scalar_mul", mu, mu, -1.0, inc=("sv", 1))
        p.op("tensor", "matmul", bca[:], ones_row_f[:], a_row,
             start=True, stop=True, waits=[("sv", c_b)])
        c_bc = p.op("tensor", "matmul", bcb[:], ones_row_f[:], mu,
                    start=True, stop=True, inc=("st", 1))
        marks = {}
        for c in range(NCH):
            w = [("st", c_bc)] if c == 0 else []
            p.op("vector", "tensor_tensor", tmpf1[:], XF[:, c * TT:(c + 1) * TT],
                 bca[:], ALU.mult, waits=w)
            inc = ("sv", 1) if c % 8 == 7 else None
            mk = p.op("vector", "tensor_tensor", XN[:, c * TT:(c + 1) * TT],
                      tmpf1[:], bcb[:], ALU.add, inc=inc)
            if c % 8 == 7:
                marks[c] = mk
        return marks

    def emit_router(ln_marks, c_id):
        lg_ps = PS[6][:, 0:32]
        rt_ps = PS[7][:].bitcast(BF16)[0:8, 0:TT]
        last_ln = ln_marks[NCH - 1]
        for ts in range(4):
            for i in range(NCH):
                w = [("sv", last_ln)] if (ts == 0 and i == 0) else []
                inc = ("st", 1) if (ts == 3 and i == NCH - 1) else None
                c_lg = p.op("tensor", "matmul", lg_ps[:, ts * 8:(ts + 1) * 8],
                            XN[:, i * TT + ts * 128: i * TT + (ts + 1) * 128],
                            gate_sb[:, i * 8:(i + 1) * 8],
                            start=(i == 0), stop=(i == NCH - 1), waits=w, inc=inc)
        c_mx = p.op("vector", "reduce_max", rsc3[:, 0:4],
                    lg_ps.rearrange("p (g e) -> p g e", e=8), AX.X,
                    waits=[("st", c_lg)], inc=("sv", 1))
        p.wait("vector", ("sv", c_mx))
        for g in range(4):
            inc = ("sv", 1) if g == 3 else None
            c_sh = p.op("vector", "tensor_scalar", rsc1[:, g * 8:(g + 1) * 8],
                        lg_ps[:, g * 8:(g + 1) * 8], rsc3[:, g:g + 1], None,
                        ALU.subtract, inc=inc)
        c_ex = p.op("scalar", "activation", rsc2[:], rsc1[:], AF.Exp,
                    waits=[("sv", c_sh)], inc=("ss", 1))
        c_sm = p.op("vector", "reduce_sum", rsc3[:, 0:4],
                    rsc2[:].rearrange("p (g e) -> p g e", e=8), AX.X,
                    waits=[("ss", c_ex)], inc=("sv", 1))
        p.wait("vector", ("sv", c_sm))
        c_rc = p.op("vector", "reciprocal", rsc3[:, 4:8], rsc3[:, 0:4], inc=("sv", 1))
        p.wait("vector", ("sv", c_rc))
        for g in range(4):
            inc = ("sv", 1) if g == 3 else None
            c_rt = p.op("vector", "tensor_scalar", routing[:, g * 8:(g + 1) * 8],
                        rsc2[:, g * 8:(g + 1) * 8], rsc3[:, 4 + g:5 + g], None,
                        ALU.mult, inc=inc)
        for g in range(4):
            w = [("sv", c_rt), ("sg", c_id)] if g == 0 else []
            inc = ("st", 1) if g == 3 else None
            c_tr = p.op("tensor", "transpose", rt_ps[:, g * 128:(g + 1) * 128],
                        routing[:, g * 8:(g + 1) * 8], ident[:], waits=w, inc=inc)
        c_rT = p.op("scalar", "activation", routingT[:], rt_ps, AF.Copy,
                    waits=[("st", c_tr)], inc=("ss", 1))
        return c_rT

    def emit_sign_chunk(o, w_sgn_free):
        spar = o % 3
        w_ld = list(w_sgn_free) + [("dSC%d" % spar, p.val("dSC%d" % spar))]
        slot = SGN[:, spar * D:(spar + 1) * D]
        c_ld = p.dma("gpsimd", slot.rearrange("p (c f) -> p c f", f=128),
                     wf_tc[o].rearrange("(c p) f -> p c f", p=128),
                     "dWf%d" % spar, waits=w_ld)
        p.op("vector", "tensor_scalar", WSTb[:, 0:D], slot, 0.0, None, ALU.is_lt,
             waits=[("dWf%d" % spar, c_ld)])
        c_sg = p.op("vector", "scalar_tensor_tensor", slot, slot, 0.0,
                    WSTb[:, 0:D], ALU.is_gt, ALU.subtract, inc=("svg", 1))
        p.dma("sync", sgn_c[o].rearrange("(c p) f -> p c f", p=128),
              slot.rearrange("p (c f) -> p c f", f=128),
              "dSC%d" % spar, waits=[("svg", c_sg)])
        return c_sg

    def emit_ffn(c_routingT, ln_marks, first_ffn, sgn_free0, sgn_free1=None, h_dst_free=()):
        sc_ps = [PS[0], PS[1]]
        y_ps = [PS[2], PS[3], PS[6]]
        os_ps = [PS[4], PS[5], PS[7]]
        h_base = p.val("sv_h")
        for i in range(NCH):
            par = i % 2
            w = [("ss", c_routingT)] if i < 2 else [("sv_h", h_base + i - 1)]
            c_mm = p.op("tensor", "matmul", sc_ps[par][:],
                        ics_sb[:, i * 128:(i + 1) * 128], routingT[:],
                        start=True, stop=True, waits=w, inc=("st_h", 1))
            w2 = [("st_h", c_mm)]
            if i % 8 == 0:
                w2.append(("sv", ln_marks[(i // 8) * 8 + 7]))
            if i == 0:
                w2 += list(h_dst_free)
            p.op("vector", "tensor_tensor", XSQ[:, i * TT:(i + 1) * TT],
                 XN[:, i * TT:(i + 1) * TT], sc_ps[par][:], ALU.mult,
                 waits=w2, inc=("sv_h", 1))
        y_base = p.val("sv_y")
        os_base = p.val("ss_os")
        y_group_done = {}
        sgn_ready = {}
        if first_ffn:
            sgn_ready[0] = emit_sign_chunk(0, sgn_free0)
            sgn_ready[1] = emit_sign_chunk(1, sgn_free0)
            sgn_ready[2] = emit_sign_chunk(2, sgn_free0)
        for o in range(NCH):
            par = o % 3
            spar = o % 3
            if first_ffn:
                sgw = [("svg", sgn_ready[o])]
            else:
                if o < 3:
                    wly = list(sgn_free0) + [("dSC0", p.val("dSC0")), ("dSC1", p.val("dSC1"))]
                else:
                    wly = [("st_y", y_group_done[o - 3])]
                c_l = p.dma("sync", SGN[:, spar * D:(spar + 1) * D]
                            .rearrange("p (c f) -> p c f", f=128),
                            sgn_c[o].rearrange("(c p) f -> p c f", p=128),
                            "dY%d" % spar, waits=wly)
                sgw = [("dY%d" % spar, c_l)]
            for i in range(NCH):
                w = list(sgw) if i == 0 else []
                if i == 0 and o == 2:
                    w.append(("ss", c_routingT))
                if o == 0:
                    w.append(("sv_h", h_base + i + 1))
                if i == 0 and o >= 3:
                    w.append(("sv_y", y_base + o - 2))
                inc = ("st_y", 1) if i == NCH - 1 else None
                c_y = p.op("tensor", "matmul", y_ps[par][:],
                           SGN[:, spar * D + i * 128: spar * D + (i + 1) * 128],
                           XSQ[:, i * TT:(i + 1) * TT],
                           start=(i == 0), stop=(i == NCH - 1), waits=w, inc=inc)
            y_group_done[o] = c_y
            w = [("ss_os", os_base + o - 2)] if o >= 3 else []
            if o == 2:
                w.append(("ss", c_routingT))
            c_os_mm = p.op("tensor", "matmul", os_ps[par][:],
                           ocs_sb[:, o * 128:(o + 1) * 128], routingT[:],
                           start=True, stop=True, waits=w, inc=("st_y", 1))
            c_os = p.op("scalar", "activation", osc_sb[:, par * TT:(par + 1) * TT],
                        os_ps[par][:], AF.Copy, waits=[("st_y", c_os_mm)],
                        inc=("ss_os", 1))
            pw = [("st_y", c_os_mm), ("ss_os", c_os)]
            if o % 8 == 0:
                pw.append(("sv", ln_marks[(o // 8) * 8 + 7]))
            p.op("vector", "tensor_tensor", tmpf2[:], y_ps[par][:],
                 osc_sb[:, par * TT:(par + 1) * TT], ALU.mult, waits=pw)
            p.op("vector", "tensor_scalar", tmpf1[:], tmpf2[:],
                 hbias_sb[:, o:o + 1], 0.5, ALU.add, ALU.mult)
            p.op("vector", "tensor_tensor", XF[:, o * TT:(o + 1) * TT],
                 XF[:, o * TT:(o + 1) * TT], tmpf1[:], ALU.add,
                 inc=("sv_y", 1))
            if first_ffn and o + 3 < NCH:
                sgn_ready[o + 3] = emit_sign_chunk(o + 3, [("st_y", y_group_done[o])])
        return p.val("sv_y"), p.val("st_y"), h_base + NCH

    def emit_phase1_tile(tile, consts):
        first = (tile == 0)
        wv = [("dXSP", p.val("dXSP"))] if tile == 1 else []
        ln1_gw = []
        for g in range(4):
            c_xg = p.dma("sync", XF[:, 8 * g * TT:8 * (g + 1) * TT]
                         .rearrange("p (c t) -> p c t", t=TT),
                         xT.rearrange("(c p) t -> p c t", p=128)
                         [:, 8 * g:8 * (g + 1), tile * TT:(tile + 1) * TT],
                         "dXL%d" % g, waits=(wv if g == 0 else []))
            ln1_gw.append([("dXL%d" % g, c_xg)])
            if tile == 1 and g == 0:
                ln1_gw[0].append(("dXS", p.val("dXS")))
                ln1_gw[0].append(("dXSP", p.val("dXSP")))
        ln1 = emit_ln(group_waits=ln1_gw)
        c_rt = emit_router(ln1, consts["c_id"])
        if first:
            sf0 = sf1 = []
        else:
            sf0 = [("st_y", consts["sgn_last_t"])]
            sf1 = [("st_y", consts["sgn_last_t"])]
        c_ffn, st_y_end, h_end = emit_ffn(c_rt, ln1, first, sf0, sf1)
        consts["sgn_last_t"] = st_y_end
        ln2 = emit_ln(group_waits=[
            [("sv_y", c_ffn - 32 + 8 * (g + 1)), ("sv_h", h_end - 32 + 8 * (g + 1))]
            for g in range(4)], sq_waits=[("st_y", st_y_end)])
        for g in range(4):
            p.dma("sync", xsp.rearrange("(c p) t -> p c t", p=128)
                  [:, 8 * g:8 * (g + 1), tile * TT:(tile + 1) * TT],
                  XF[:, 8 * g * TT:8 * (g + 1) * TT].rearrange("p (c t) -> p c t", t=TT),
                  "dXSP", waits=[("sv_y", c_ffn - 32 + 8 * (g + 1))])
        c_ag = p.dma("sync", agx_in[tile].rearrange("(c p) t -> p c t", p=128),
                     XN[:].rearrange("p (c t) -> p c t", t=TT), "dXS",
                     waits=[("sv", ln2[NCH - 1])])
        p.op("gpsimd", "collective_compute", "AllGather", ALU.bypass,
             replica_groups=[list(range(NC))],
             ins=[agx_in[tile]], outs=[agx_out[tile]],
             waits=[("dXS", c_ag)], inc=("ccx%d" % tile, 1))

    def emit_phase3(consts):
        q_ps = [PS[0], PS[1]]
        k_ps = [PS[2], PS[3]]
        v_ps = [PS[4], PS[5]]

        def xwin_chunk(slot, i, sub=None):
            if slot == 0:
                ap = XFb[:, i * TT:(i + 1) * TT]
            elif slot == 1:
                ap = XFb[:, NCH * TT + i * TT: NCH * TT + (i + 1) * TT]
            elif i < 24:
                ap = SGN[:, i * TT:(i + 1) * TT]
            else:
                ap = WSTb[:, (i - 24) * TT:(i - 23) * TT]
            if sub is not None:
                ap = ap[:, sub * 128:(sub + 1) * 128]
            return ap

        def load_window(slot, b, r, waits):
            if slot < 2:
                dst = XFb[:, slot * NCH * TT:(slot + 1) * NCH * TT]
                c = p.dma("scalar", dst.rearrange("p (c t) -> p c t", t=TT),
                          agx_out[b].rearrange("(r c p) t -> r p c t", p=128, c=NCH)[r],
                          "dWN%d" % slot, waits=waits)
            else:
                srcv = agx_out[b].rearrange("(r c p) t -> r p c t", p=128, c=NCH)[r]
                p.dma("scalar", SGN[:].rearrange("p (c t) -> p c t", t=TT),
                      srcv[:, 0:24], "dWN2", waits=waits)
                c = p.dma("scalar", WSTb[:].rearrange("p (c t) -> p c t", t=TT),
                          srcv[:, 24:32], "dWN2")
            return c

        wv_sb = XN[:, 0:NCH * 512]
        c_wv = p.dma("gpsimd", wv_sb.rearrange("p (c f) -> p c f", f=512),
                     wv_t.rearrange("(c p) f -> p c f", p=128), "dG",
                     waits=[("dXS", p.val("dXS"))])
        for hp in range(2):
            hsl = slice(hp * 256, (hp + 1) * 256)
            wq_sb = XSQ[:, 0:NCH * 256]
            wk_sb = XSQ[:, NCH * 256:2 * NCH * 256]
            w0 = []
            if hp == 1:
                w0 = [("st3", p.val("st3"))]
            c_wq = p.dma("gpsimd", wq_sb.rearrange("p (c f) -> p c f", f=256),
                         wq_t[:, hsl].rearrange("(c p) f -> p c f", p=128), "dG",
                         waits=w0)
            c_wk = p.dma("gpsimd", wk_sb.rearrange("p (c f) -> p c f", f=256),
                         wk_t[:, hsl].rearrange("(c p) f -> p c f", p=128), "dG")
            widx = 0
            for b in range(2):
                for r in range(NC):
                    par = (hp * 16 + widx) % 3
                    w = [("ccx%d" % b, 1)]
                    st3w_now = p.val("st3w")
                    if hp * 16 + widx >= 3:
                        w.append(("st3w", st3w_now - 2))
                    if hp * 16 + widx == 16:
                        pass
                    c_win = load_window(par, b, r, w)
                    gtok = r * TPC + b * TT
                    for hl in range(2):
                        h = hp * 2 + hl
                        qp = hl % 2
                        n_qk = p.val("ss3q")
                        w0 = [("dWN%d" % par, c_win)]
                        if hp == 0 and widx == 0 and hl == 0:
                            w0.append(("dG", c_wk))
                        if n_qk >= 2:
                            w0.append(("ss3q", n_qk - 1))
                        for i in range(NCH):
                            w_ = w0 if i == 0 else []
                            inc = ("st3", 1) if i == NCH - 1 else None
                            c_q = p.op("tensor", "matmul", q_ps[qp][:],
                                       wq_sb[:, i * 256 + hl * 128: i * 256 + (hl + 1) * 128],
                                       xwin_chunk(par, i),
                                       start=(i == 0), stop=(i == NCH - 1),
                                       waits=w_, inc=inc)
                        n_k = p.val("ss3k")
                        w0 = [("ss3k", n_k - 1)] if n_k >= 2 else []
                        for i in range(NCH):
                            w_ = w0 if i == 0 else []
                            inc = ("st3", 1) if i == NCH - 1 else None
                            c_k = p.op("tensor", "matmul", k_ps[qp][:],
                                       wk_sb[:, i * 256 + hl * 128: i * 256 + (hl + 1) * 128],
                                       xwin_chunk(par, i),
                                       start=(i == 0), stop=(i == NCH - 1),
                                       waits=w_, inc=inc)
                        sl = p.val("ss3q") % 2
                        c_eq = p.op("vector", "tensor_scalar", stq[:, sl * 512:(sl + 1) * 512],
                                    q_ps[qp][:], bq_sb[:, h:h + 1], None, ALU.add,
                                    waits=[("st3", c_q), ("dQ%d" % sl, p.val("dQ%d" % sl))],
                                    inc=("ss3q", 1))
                        p.dma("sync", qsp[h][:, gtok:gtok + TT],
                              stq[:, sl * 512:(sl + 1) * 512], "dQ%d" % sl,
                              waits=[("ss3q", c_eq)])
                        sl = p.val("ss3k") % 2
                        c_ek = p.op("vector", "tensor_scalar", stk[:, sl * 512:(sl + 1) * 512],
                                    k_ps[qp][:], bk_sb[:, h:h + 1], None, ALU.add,
                                    waits=[("st3", c_k), ("dK%d" % sl, p.val("dK%d" % sl))],
                                    inc=("ss3k", 1))
                        p.dma("sync", ksp[h][:, gtok:gtok + TT],
                              stk[:, sl * 512:(sl + 1) * 512], "dK%d" % sl,
                              waits=[("ss3k", c_ek)])
                    if hp == 0:
                        for tsl in range(4):
                            vp = tsl % 2
                            n_v = p.val("sv3")
                            w0 = [("dWN%d" % par, c_win)]
                            if widx == 0 and tsl == 0:
                                w0.append(("dG", c_wv))
                            if n_v >= 2:
                                w0.append(("sv3", n_v - 1))
                            for i in range(NCH):
                                w_ = w0 if i == 0 else []
                                inc = ("st3", 1) if i == NCH - 1 else None
                                c_v = p.op("tensor", "matmul", v_ps[vp][:],
                                           xwin_chunk(par, i, sub=tsl),
                                           wv_sb[:, i * 512:(i + 1) * 512],
                                           start=(i == 0), stop=(i == NCH - 1),
                                           waits=w_, inc=inc)
                            sl = p.val("sv3") % 2
                            c_ev = p.op("vector", "tensor_tensor", stv[:, sl * 512:(sl + 1) * 512],
                                        v_ps[vp][:], bvb_sb[:], ALU.add,
                                        waits=[("st3", c_v), ("dV%d" % sl, p.val("dV%d" % sl))],
                                        inc=("sv3", 1))
                            gt = gtok + tsl * 128
                            p.dma("sync",
                                  vsp[:, gt:gt + 128, :].rearrange("h t d -> t h d"),
                                  stv[:, sl * 512:(sl + 1) * 512]
                                  .rearrange("p (h d) -> p h d", d=HD),
                                  "dV%d" % sl, waits=[("sv3", c_ev)])
                    p.op("tensor", "nop", inc=("st3w", 1))
                    widx += 1
        return {"dQ": (p.val("dQ0"), p.val("dQ1")),
                "dK": (p.val("dK0"), p.val("dK1")),
                "dV": (p.val("dV0"), p.val("dV1"))}

    def emit_phase4(p3, consts):
        slots = {
            0: {"K": XFb[:, 0:8192], "Q": XFb[:, 8192:16384], "V": XFb[:, 16384:24576]},
            1: {"K": XSQ[:, 0:8192], "Q": XSQ[:, 8192:16384], "V": XFb[:, 24576:32768]},
        }
        qsq = XN[:, 0:8192]
        ppp = [XN[:, 8192 + j * 1024: 8192 + (j + 1) * 1024] for j in range(3)]
        sta = [stq[:, 0:512], stq[:, 512:1024]]
        sp_ps = [PSQ[0], PSQ[1], PSQ[2]]
        o_ps = PS[6]
        dF = PS[7]
        f_ps = PS[7]
        n_ps2 = [PSQ[0][0:1, 0:512], PSQ[0][0:1, 512:1024]]

        head_done = {}
        hload = {}

        def load_head(h):
            par = h % 2
            if h < 2:
                w = [("dQ0", p3["dQ"][0]), ("dQ1", p3["dQ"][1]),
                     ("dK0", p3["dK"][0]), ("dK1", p3["dK"][1]),
                     ("dV0", p3["dV"][0]), ("dV1", p3["dV"][1])]
            else:
                w = [("st4", head_done[h - 2])]
            p.dma("sync", slots[par]["K"], ksp[h], "dH%d" % par, waits=w)
            p.dma("sync", slots[par]["Q"], qsp[h], "dH%d" % par)
            p.dma("sync", slots[par]["V"].rearrange("p (tt d) -> p tt d", d=128),
                  vsp[h].rearrange("(tt p) d -> p tt d", p=128), "dH%d" % par)
            return p.val("dH%d" % par)

        hload[0] = load_head(0)
        hload[1] = load_head(1)
        sp_led = [0, 0, 0]
        pp_led = [0, 0, 0]
        f_led = 0
        t1_led = 0
        t2_led = [0]
        o_led = ("sv4", 0)
        d_led = 0
        n_led = [0, 0]
        for h in range(HPC):
            par = h % 2
            sl = slots[par]
            for qk in range(2):
                src = sl["Q"] if qk == 0 else sl["K"]
                w = [("dH%d" % par, hload[h]), ("st4n", p.val("st4n"))]
                c_sq2 = p.op("scalar", "activation", qsq, src, AF.Square,
                             waits=w, inc=("ss4", 1))
                for w16 in range(16):
                    nb = w16 % 2
                    ww = [("ss4", c_sq2)] if w16 == 0 else []
                    ww.append(("sv4", n_led[nb]))
                    if w16 == 0 and qk == 0:
                        ww.append(("ss4", sp_led[0]))
                    c_nm = p.op("tensor", "matmul", n_ps2[nb], ones_col[:],
                                qsq[:, w16 * 512:(w16 + 1) * 512],
                                start=True, stop=True,
                                waits=ww, inc=("st4n", 1))
                    n_led[nb] = p.op("vector", "reduce_max",
                                     rowsB[:, 1024 + w16:1025 + w16],
                                     n_ps2[nb], AX.X, waits=[("st4n", c_nm)],
                                     inc=("sv4", 1))
                p.wait("vector", ("sv4", max(n_led)))
                c_mx4 = p.op("vector", "reduce_max", rowsB[:, 1040 + qk:1041 + qk],
                             rowsB[:, 1024:1040], AX.X, inc=("sv4", 1))
                p.wait("vector", ("sv4", c_mx4))
                if _DEV_STAGE == "attn_dbg":
                    p.dma("sync", dbg_nrm[h][qk:qk + 1, 0:20], rowsB[:, 1024:1044],
                          "dDBG", waits=[("sv4", c_mx4)])
                    if qk == 0:
                        p.dma("sync", dbg_qsq[h], qsq[:, 0:512], "dDBG",
                              waits=[("sv4", c_mx4)])
                        p.dma("sync", dbg_qslot[h], sl["Q"][:, 0:512], "dDBG",
                              waits=[("sv4", c_mx4)])
            c_pr = p.op("vector", "tensor_mul", rowsB[:, 1042:1043],
                        rowsB[:, 1040:1041], rowsB[:, 1041:1042], inc=("sv4", 1))
            p.wait("vector", ("sv4", c_pr))
            c_lnn = p.op("scalar", "activation", rowsB[:, 1042:1043],
                         rowsB[:, 1042:1043], AF.Ln, waits=[("sv4", c_pr)],
                         inc=("ss4", 1))
            p.wait("scalar", ("ss4", c_lnn))
            c_sqr = p.op("scalar", "activation", rowsB[:, 1042:1043],
                         rowsB[:, 1042:1043], AF.Exp, scale=0.5, inc=("ss4", 1))
            c_nc = p.op("vector", "tensor_scalar", rowsB[:, 1042:1043],
                        rowsB[:, 1042:1043], -ISQ, -1.0, ALU.mult, ALU.add,
                        waits=[("ss4", c_sqr)], inc=("sv4", 1))
            c_bcc = p.op("tensor", "matmul", f_ps[:, 0:1], ones_row_f[:],
                         rowsB[:, 1042:1043], start=True, stop=True,
                         waits=[("sv4", c_nc), ("ss4f", f_led), ("sv4", d_led)],
                         inc=("st4f", 1))
            c_negc = p.op("scalar", "activation", negC[:], f_ps[:, 0:1], AF.Copy,
                          waits=[("st4f", c_bcc)], inc=("ss4f", 1))
            f_led = c_negc
            p.wait("scalar", ("ss4f", c_negc))
            if _DEV_STAGE == "attn_dbg":
                p.dma("sync", dbg_nc[h], negC[0:1, 0:1], "dDBG",
                      waits=[("ss4f", c_negc)])
            if h >= 1:
                p.wait("vector", ("dA%d_0" % (h - 1), p.val("dA%d_0" % (h - 1))),
                       ("dA%d_1" % (h - 1), p.val("dA%d_1" % (h - 1))))
            for qb in range(16):
                nkt = 4 * (qb + 1)
                npair = nkt // 2
                smm = {}
                avw = {}

                def emit_spair(j):
                    pb = j % 3
                    w = [("ss4", sp_led[pb])]
                    if j == 0:
                        w.append(("dH%d" % par, hload[h]))
                        if qb == 0:
                            w.append(("sv4", max(n_led)))
                    p.op("tensor", "matmul", sp_ps[pb][:, 0:512],
                         sl["K"][:, (2 * j) * 128:(2 * j + 1) * 128],
                         sl["Q"][:, qb * 512:(qb + 1) * 512],
                         start=True, stop=True, waits=w)
                    smm[j] = p.op("tensor", "matmul", sp_ps[pb][:, 512:1024],
                                  sl["K"][:, (2 * j + 1) * 128:(2 * j + 2) * 128],
                                  sl["Q"][:, qb * 512:(qb + 1) * 512],
                                  start=True, stop=True, inc=("st4s", 1))

                def emit_exppair(j):
                    pb = j % 3
                    w = [("st4s", smm[j]), ("st4", pp_led[pb]), ("ss4f", c_negc)]
                    c_e = p.op("scalar", "activation", ppp[pb][:],
                               sp_ps[pb][:], AF.Exp, bias=negC[:, 0:1],
                               scale=ISQ, waits=w, inc=("ss4", 1))
                    sp_led[pb] = c_e
                    for hh in range(2):
                        kt = 2 * j + hh
                        if kt >= 4 * qb:
                            c_m = p.op("gpsimd", "affine_select",
                                       ppp[pb][:, hh * 512:(hh + 1) * 512],
                                       ppp[pb][:, hh * 512:(hh + 1) * 512],
                                       [[1, 512]], ALU.is_ge, 0.0,
                                       base=qb * 512 - kt * 128,
                                       channel_multiplier=-1,
                                       waits=[("ss4", c_e)], inc=("sg4", 1))
                            avw[kt] = ("sg4", c_m)
                        else:
                            avw[kt] = ("ss4", c_e)

                emit_spair(0)
                emit_exppair(0)
                if npair > 1:
                    emit_spair(1)
                    emit_exppair(1)
                c_den = None
                for j in range(npair):
                    pb = j % 3
                    if j + 2 < npair:
                        emit_spair(j + 2)
                        emit_exppair(j + 2)
                    for hh in range(2):
                        kt = 2 * j + hh
                        pphalf = ppp[pb][:, hh * 512:(hh + 1) * 512]
                        w = [avw[kt]]
                        if kt == 0:
                            w.append(o_led)
                        p.op("tensor", "matmul", o_ps[:],
                             sl["V"][:, kt * 128:(kt + 1) * 128],
                             pphalf, start=(kt == 0), stop=(kt == nkt - 1), waits=w)
                        w = []
                        if kt == 0:
                            w = [("sv4", d_led), ("ss4f", f_led)]
                        c_den = p.op("tensor", "matmul", dF[:], onesM[:],
                                     pphalf,
                                     start=(kt == 0), stop=(kt == nkt - 1),
                                     waits=w, inc=("st4", 1))
                    pp_led[pb] = c_den
                c_oev = p.op("scalar", "activation", tmpf2[:], o_ps[:], AF.Copy,
                             waits=[("st4", c_den), ("sv4", t2_led[0])],
                             inc=("ss4f", 1))
                o_led = ("ss4f", c_oev)
                c_rc = p.op("vector", "reciprocal", rowsA[:, 1024:1536],
                            dF[0:1, :], waits=[("st4", c_den)], inc=("sv4", 1))
                d_led = c_rc
                if _DEV_STAGE == "attn_dbg":
                    p.dma("sync", dbg_den[h][qb:qb + 1, :], rowsA[:, 1024:1536],
                          "dDBG", waits=[("sv4", c_rc)])
                c_fb = p.op("tensor", "matmul", f_ps[:], ones_row_f[:],
                            rowsA[:, 1024:1536], start=True, stop=True,
                            waits=[("sv4", c_rc), ("ss4f", f_led)], inc=("st4f", 1))
                c_cp = p.op("scalar", "activation", tmpf1[:], f_ps[:], AF.Copy,
                            waits=[("st4f", c_fb), ("sv4", t1_led)], inc=("ss4f", 1))
                f_led = c_cp
                ssl = qb % 2
                sA = "dA%d_%d" % (h, ssl)
                c_tt = p.op("vector", "tensor_tensor", sta[ssl], tmpf2[:], tmpf1[:],
                            ALU.mult,
                            waits=[("ss4f", c_cp), (sA, p.val(sA))],
                            inc=("sv4", 1))
                t1_led = c_tt
                t2_led[0] = c_tt
                p.dma("sync", aga_in[h * 128:(h + 1) * 128, qb * 512:(qb + 1) * 512],
                      sta[ssl], sA, waits=[("sv4", c_tt)])
            head_done[h] = p.val("st4")
            if h + 2 < HPC:
                hload[h + 2] = load_head(h + 2)
            p.op("gpsimd", "collective_compute", "AllGather", ALU.bypass,
                 replica_groups=[list(range(NC))],
                 ins=[aga_in[h * 128:(h + 1) * 128, :]], outs=[aga_out[h]],
                 waits=[("dA%d_0" % h, p.val("dA%d_0" % h)),
                        ("dA%d_1" % h, p.val("dA%d_1" % h))],
                 inc=("cca%d" % h, 1))

    def emit_phase6_tile(tile, consts):
        w = []
        if tile == 1:
            w = [("dOUT", consts["c_out0"])]
        c_x = p.dma("sync", XF[:].rearrange("p (c t) -> p c t", t=TT),
                    xsp.rearrange("(c p) t -> p c t", p=128)[:, :, tile * TT:(tile + 1) * TT],
                    "dXL2", waits=w)
        ww = [("cca%d" % hh, 1) for hh in range(HPC)]
        if tile == 1:
            ww = [("st6", consts["st6_0"]), ("st", p.val("st")),
                  ("st_y", consts["sgn_last_t"])]
        xsq_v = XSQ[:].rearrange("p (r four t) -> p r four t", four=HPC, t=TT)
        for hl in range(HPC):
            w_ = ww if hl == 0 else []
            wl = [(p.sem(n), v) for (n, v) in w_ if v > 0]
            dst = xsq_v[:, :, hl, :]
            sem = p.sem("dAT")
            p.inc("dAT", 16)

            def f(e, wl=wl, dst=dst, hl=hl, tile=tile, sem=sem):
                for s_, v_ in wl:
                    e.wait_ge(s_, v_)
                if ("idx%d" % tile) not in p.state:
                    p.state["idx%d" % tile] = e.snap(p.state["cid"] * 2 + tile)
                idx = p.state["idx%d" % tile]
                src = aga_out[hl].rearrange("(r p) s -> p r s", p=128)[:, :, bass.ts(idx, TT)]
                e.dma_start(out=dst, in_=src).then_inc(sem, 16)
            p.raw("sync", f)
        c_at = p.val("dAT")
        y_ps = [PS[0], PS[1], PS[2]]
        ob = p.val("sv_o")
        st6_base = p.val("st6")
        y6_done = {}
        for o in range(NCH):
            par = o % 3
            spar = o % 3
            if o < 3:
                wol = list(consts["sgn_free6"][min(o, len(consts["sgn_free6"]) - 1)])
            else:
                wol = [("st6", y6_done[o - 3])]
            c_wo = p.dma("gpsimd", SGN[:, spar * D:(spar + 1) * D]
                         .rearrange("p (c f) -> p c f", f=128),
                         wo_tc[o].rearrange("(c p) f -> p c f", p=128),
                         "dWO%d" % spar, waits=wol)
            for i in range(NCH):
                w_ = [("dWO%d" % spar, c_wo), ("dAT", c_at)] if i == 0 else []
                if i == 0 and o >= 3:
                    w_.append(("sv_o", ob + o - 2))
                inc = ("st6", 1) if i == NCH - 1 else None
                c_y = p.op("tensor", "matmul", y_ps[par][:],
                           SGN[:, spar * D + i * 128: spar * D + (i + 1) * 128],
                           XSQ[:, i * TT:(i + 1) * TT],
                           start=(i == 0), stop=(i == NCH - 1), waits=w_, inc=inc)
            y6_done[o] = c_y
            w_ = [("st6", c_y)]
            if o == 0:
                w_.append(("dXL2", c_x))
            p.op("vector", "scalar_tensor_tensor", XF[:, o * TT:(o + 1) * TT],
                 y_ps[par][:], bo_sb[:, o:o + 1], XF[:, o * TT:(o + 1) * TT],
                 ALU.add, ALU.add, waits=w_, inc=("sv_o", 1))
        consts["st6_0"] = p.val("st6")
        if _DEV_STAGE in ("attn", "attn_dbg"):
            c_out = p.dma("sync", yT.rearrange("(c p) t -> p c t", p=128)[:, :, tile * TT:(tile + 1) * TT],
                          XF[:].rearrange("p (c t) -> p c t", t=TT), "dOUT",
                          waits=[("sv_o", p.val("sv_o"))])
            consts["c_out0"] = c_out
            consts["sgn_free6"] = [[("st6", y6_done[NCH - 2])], [("st6", y6_done[NCH - 1])]]
            consts["sgn_last_t"] = p.val("st_y")
            return c_out
        svo = p.val("sv_o")
        ln3 = emit_ln(group_waits=[[("sv_o", svo - 32 + 8 * (g + 1))] for g in range(4)],
                      sq_waits=[("st6", y6_done[NCH - 1])])
        if _DEV_STAGE == "full_dbg":
            p.dma("sync", dbg_x3[tile], XF[:], "dDBG", waits=[("sv", ln3[NCH - 1])])
            p.dma("sync", dbg_xn3[tile], XN[:], "dDBG", waits=[("sv", ln3[NCH - 1])])
        c_rt = emit_router(ln3, consts["c_id"])
        if _DEV_STAGE == "full_dbg":
            p.dma("sync", dbg_rt[tile], routingT[:], "dDBG", waits=[("ss", c_rt)])
        sf0 = [("st6", y6_done[NCH - 1])]
        sf1 = sf0
        c_ffn, st_y_end, _h6 = emit_ffn(c_rt, ln3, False, sf0, sf1,
                                        h_dst_free=[("st6", y6_done[NCH - 1])])
        consts["sgn_last_t"] = st_y_end
        consts["sgn_free6"] = [[("st_y", st_y_end)]] * 3
        c_out = p.dma("sync", yT.rearrange("(c p) t -> p c t", p=128)[:, :, tile * TT:(tile + 1) * TT],
                      XF[:].rearrange("p (c t) -> p c t", t=TT), "dOUT",
                      waits=[("sv_y", c_ffn)])
        consts["c_out0"] = c_out
        return c_out

    with es:
        with nc.Block() as block:
            consts = phase0()
            p.fence()
            emit_phase1_tile(0, consts)
            emit_phase1_tile(1, consts)
            if _DEV_STAGE == "ffn1":
                c_o = p.dma("sync", yT.rearrange("(c p) t -> p c t", p=128),
                            xsp.rearrange("(c p) t -> p c t", p=128), "dOUT",
                            waits=[("dXS", p.val("dXS"))])
                p.wait("sync", ("dOUT", c_o))
            else:
                p.fence()
                p3 = emit_phase3(consts)
                p.fence()
                emit_phase4(p3, consts)
                p.fence()
                if _DEV_STAGE == "attn_dbg":
                    p.dma("sync", dbg_q[:, :, :], qsp[:, :, :], "dDBG")
                    p.dma("sync", dbg_k[:, :, :], ksp[:, :, :], "dDBG")
                    p.dma("sync", dbg_v[:, :, :], vsp[:, :, :], "dDBG")
                    p.dma("sync", dbg_ain[:, :], aga_in[:, :], "dDBG")
                    p.dma("sync", dbg_aout[:, :, :], aga_out[:, :, :], "dDBG")
                    p.dma("sync", dbg_xn2[:, 0:TT], agx_out[0][:, :], "dDBG")
                    c_dbg = p.dma("sync", dbg_xn2[:, TT:2 * TT], agx_out[1][:, :], "dDBG")
                    p.wait("sync", ("dDBG", c_dbg))
                consts["sgn_free6"] = [[], [], []]
                emit_phase6_tile(0, consts)
                emit_phase6_tile(1, consts)
                p.wait("sync", ("dOUT", p.val("dOUT")))

            for eng_name in ENGS:
                def make_body(name):
                    def body(e):
                        for f in p.streams[name]:
                            f(e)
                    return body
                getattr(block, eng_name)(make_body(eng_name))
    return nc


def kernel(L0_attn_x, L0_attn_Wq, L0_attn_bq, L0_attn_Wk, L0_attn_bk,
           L0_attn_Wv, L0_attn_bv, L0_attn_Wo, L0_attn_bo,
           L0_ffn_weight, L0_ffn_bias, L0_ffn_gate_w,
           L0_ffn_in_channel_scale, L0_ffn_out_channel_scale):
    x = np.asarray(L0_attn_x, np.float32).reshape(S, D)
    xTfull = np.ascontiguousarray(x.T)
    wq_T = np.ascontiguousarray(np.asarray(L0_attn_Wq, np.float32).T)
    wk_T = np.ascontiguousarray(np.asarray(L0_attn_Wk, np.float32).T)
    wv_T = np.ascontiguousarray(np.asarray(L0_attn_Wv, np.float32).T)
    wo_T = np.asarray(L0_attn_Wo, np.float32).T
    wf_T = np.asarray(L0_ffn_weight, np.float32).T
    wo_tc = np.ascontiguousarray(wo_T.reshape(D, NCH, 128).transpose(1, 0, 2))
    wf_tc = np.ascontiguousarray(wf_T.reshape(D, NCH, 128).transpose(1, 0, 2))
    gate_T = np.ascontiguousarray(np.asarray(L0_ffn_gate_w, np.float32).T)
    ics = np.ascontiguousarray(np.asarray(L0_ffn_in_channel_scale, np.float32))
    ocs = np.ascontiguousarray(np.asarray(L0_ffn_out_channel_scale, np.float32))
    bq = np.asarray(L0_attn_bq, np.float32)
    bk = np.asarray(L0_attn_bk, np.float32)
    bv = np.asarray(L0_attn_bv, np.float32)
    bo = np.asarray(L0_attn_bo, np.float32)
    bfb = np.asarray(L0_ffn_bias, np.float32)
    bo_f = np.ascontiguousarray(bo.reshape(NCH, 128).T)
    bias_f = np.ascontiguousarray(bfb.reshape(NCH, 128).T)

    in_maps = []
    for r in range(NC):
        hs = slice(r * HPC * HD, (r + 1) * HPC * HD)
        in_maps.append({
            "xT": np.ascontiguousarray(xTfull[:, r * TPC:(r + 1) * TPC]),
            "wq_t": np.ascontiguousarray(wq_T[:, hs]),
            "wk_t": np.ascontiguousarray(wk_T[:, hs]),
            "wv_t": np.ascontiguousarray(wv_T[:, hs]),
            "wo_tc": wo_tc,
            "wf_tc": wf_tc,
            "gate_t": gate_T,
            "icsM": ics,
            "ocsM": ocs,
            "bq_f": np.ascontiguousarray(bq[hs].reshape(HPC, 128).T),
            "bk_f": np.ascontiguousarray(bk[hs].reshape(HPC, 128).T),
            "bv_r": np.ascontiguousarray(bv[hs].reshape(1, HPC * HD)),
            "bo_f": bo_f,
            "bias_f": bias_f,
            "cid": np.array([[r]], dtype=np.int32),
        })
    nc = build_program()
    res = run_bass_kernel_spmd(nc, in_maps, core_ids=list(range(NC)))
    out = np.empty((S, D), np.float32)
    for r in range(NC):
        out[r * TPC:(r + 1) * TPC, :] = np.asarray(res.results[r]["yT"]).T
    return out.reshape(1, S, D)
